# version 01
# baseline (speedup 1.0000x reference)
import numpy as np
import ml_dtypes
import concourse.bass as bass
import concourse.mybir as mybir
from concourse.bass_utils import run_bass_kernel_spmd

F32 = mybir.dt.float32
BF16 = mybir.dt.bfloat16
AF = mybir.ActivationFunctionType
ALU = mybir.AluOpType
AX = mybir.AxisListType
BF = ml_dtypes.bfloat16

NCORES = 8
SEQS = 4
N = 2048
D = 1024
CH = D // 128
NT = N // 128
NSL = N // 512
S_L0 = float((64.0 ** 0.5) ** -0.25)
SCALE_L1 = float(64.0 ** -0.5)
EPS = 1e-5
E0, F0 = 14, 2048
E1, I1 = 4, 4096
TB = 1024
NTB = N // TB


class Sem:
    __slots__ = ("h", "n", "name")

    def __init__(self, nc, name):
        self.h = nc.alloc_semaphore(name)
        self.n = 0
        self.name = name


class Ctx:

    def __init__(self, nc):
        self.nc = nc
        self.q = {e: [] for e in ("t", "v", "s", "g", "y")}
        self.st = Sem(nc, "st")
        self.sv = Sem(nc, "sv")
        self.ss = Sem(nc, "ss")
        self.sg = Sem(nc, "sg")
        self.sy = Sem(nc, "sy")
        self.glanes = [Sem(nc, f"gl{i}") for i in range(8)]
        self.ylanes = [Sem(nc, f"yl{i}") for i in range(4)]
        self._gi = 0
        self._yi = 0

    def emit(self, eng, build, waits=(), inc=None, inck=1):
        tgt = None
        if inc is not None:
            inc.n += inck
            tgt = inc.n
        self.q[eng].append((build, tuple((s.h, v) for s, v in waits if v > 0),
                            (inc.h if inc else None), inck))
        return tgt

    def dma(self, eng, build, waits=()):
        if eng == "g":
            lanes = self.glanes
            self._gi += 1
            lane = lanes[self._gi % len(lanes)]
        else:
            lanes = self.ylanes
            self._yi += 1
            lane = lanes[self._yi % len(lanes)]
        w = list(waits)
        if lane.n:
            w.append((lane, lane.n))
        tgt = self.emit(eng, build, waits=w, inc=lane, inck=16)
        return (lane, tgt)

    def all_sems(self):
        return [self.st, self.sv, self.ss, self.sg, self.sy] + \
            self.glanes + self.ylanes

    def barrier(self, engines="tvsgy"):
        sems = self.all_sems()
        snap = [(s, s.n) for s in sems if s.n > 0]
        for e in engines:
            own = {"t": self.st, "v": self.sv, "s": self.ss}.get(e)
            waits = tuple((s.h, v) for s, v in snap if s is not own)
            if waits:
                self.q[e].append((None, waits, None, 0))

    def finalize(self, block):
        def drain(engine, lst):
            for build, waits, inch, inck in lst:
                for h, v in waits:
                    engine.wait_ge(h, v)
                if build is not None:
                    ins = build(engine)
                    if inch is not None:
                        ins.then_inc(inch, inck)

        for key, meth in (("t", block.tensor), ("v", block.vector),
                          ("s", block.scalar), ("g", block.gpsimd),
                          ("y", block.sync)):
            lst = self.q[key]
            if lst:
                meth(lambda e, lst=lst: drain(e, lst))


def ts(i, k):
    return slice(i * k, (i + 1) * k)


def make_const_data():
    c = {}
    c["ident_bf"] = np.eye(128, dtype=BF)
    c["ident_f32"] = np.eye(128, dtype=np.float32)
    c["ones_col"] = np.ones((128, 1), dtype=BF)
    c["ones_row"] = np.ones((128, 128), dtype=BF)
    h2 = np.zeros((2, 128), dtype=BF)
    h2[0, 0:64] = 1
    h2[1, 64:128] = 1
    c["half2"] = h2
    u16 = np.zeros((16, 16 * 128), dtype=BF)
    for e in range(16):
        u16[e, ts(e, 128)] = 1
    c["u16"] = u16
    u8 = np.zeros((8, 8 * 128), dtype=BF)
    for e in range(8):
        u8[e, ts(e, 128)] = 1
    c["u8"] = u8
    return c


class B:

    def __init__(self):
        self.nc = bass.Bass("TRN2", target_bir_lowering=False)
        nc = self.nc
        self.ctx = Ctx(nc)
        self.dram = {}
        self.cdata = make_const_data()
        self.cdram = {k: nc.inline_tensor(v, "c_" + k).ap()
                      for k, v in self.cdata.items()}

        A = nc.alloc_sbuf_tensor
        self.xn = A("xn", [128, CH * N], BF16).ap()
        self.sb_s = A("sb_s", [128, CH * N], BF16).ap()
        self.sb_g = A("sb_g", [128, CH * N], BF16).ap()
        self.big = A("big", [128, 2 * CH * N], BF16).ap()
        self.ident_bf = A("t_ident_bf", [128, 128], BF16).ap()
        self.ident_f32 = A("t_ident_f32", [128, 128], F32).ap()
        self.ones_col = A("t_ones_col", [128, 1], BF16).ap()
        self.ones_row = A("t_ones_row", [128, 128], BF16).ap()
        self.half2 = A("t_half2", [2, 128], BF16).ap()
        self.u16 = A("t_u16", [16, 16 * 128], BF16).ap()
        self.u8 = A("t_u8", [8, 8 * 128], BF16).ap()
        self.colAB = A("colAB", [128, 64], F32).ap()
        self.colT = A("colT", [16, 128], BF16).ap()
        self.rwt = A("rwt", [128, 14 * 8], BF16).ap()
        self.rwt1 = A("rwt1", [128, 4 * 8], BF16).ap()
        self.bias_store = A("bias_store", [128, 768], F32).ap()
        self._bias_off = 0
        self.bias = {}
        self.brow_store = A("brow_store", [128, 2048], BF16).ap()
        self._brow_slot = 0
        self.brow = {}
        r0, _ = nc.bump_sbuf(28 * 1024)
        self.Roff = r0

        def at(name, shape, dtype, off_b):
            return nc.alloc_sbuf_tensor_at(
                name, shape, dtype, offset=r0 + off_b).ap()

        KB = 1024
        self.xb0 = at("xb0", [128, 2048], BF16, 0)
        self.xb1 = at("xb1", [128, 2048], BF16, 4 * KB)
        self.xsq0 = at("xsq0", [128, 2048], BF16, 8 * KB)
        self.xsq1 = at("xsq1", [128, 2048], BF16, 12 * KB)
        self.bcmu = at("bcmu", [128, 2048], BF16, 8 * KB)
        self.bcrs = at("bcrs", [128, 2048], BF16, 12 * KB)
        self.scr32 = at("scr32", [128, 2048], F32, 16 * KB)
        self.tokA = at("tokA", [128, 1024], BF16, 0)
        self.tokB = at("tokB", [128, 1024], BF16, 2 * KB)
        self.tokC_v = at("tokC_v", [128, 1040], BF16, 4 * KB)
        self.tokD = at("tokD", [128, 1024], F32, int(6.25 * KB))
        self.ctx2 = at("ctx2", [128, 1024], BF16, int(10.25 * KB))
        self.c_cols = at("c_cols", [128, 8], F32, int(12.5 * KB))
        self.wrol = at("wrol", [128, 4096], BF16, 16 * KB)
        self.coefstore = at("coefstore", [128, 128], F32, 20 * KB)
        self.coefbc0 = at("coefbc0", [128, 1024], BF16, 8 * KB)
        self.coefbc1 = at("coefbc1", [128, 1024], BF16, 10 * KB)
        self.lgt = at("lgt", [128, 256], F32, 12 * KB)
        self.lgt2 = at("lgt2", [128, 256], F32, 13 * KB)
        self.kk = at("kk", [128, 2048], BF16, 0)
        self.vaug = at("vaug", [128, 16 * 72], BF16, 4 * KB)
        self.pt0 = at("pt0", [128, 2048], BF16, 8 * KB)
        self.pt1 = at("pt1", [128, 2048], BF16, 12 * KB)
        self.recbc = at("recbc", [128, 2048], BF16, 8 * KB)
        self.x40 = at("x40", [128, 2048], F32, 0)
        self.x41 = at("x41", [128, 2048], F32, 8 * KB)
        self.sg4 = at("sg4", [128, 2048], BF16, 16 * KB)
        self.tmp4 = at("tmp4", [128, 2048], BF16, 20 * KB)
        self.vf32 = at("vf32", [128, 2048], F32, 8 * KB)
        self.p7_coefstore = at("p7_coefstore", [128, 128], F32, 0)
        self.p7_lgt = at("p7_lgt", [128, 256], F32, 512)
        self.p7_bc = [at("p7_bc0", [128, 1024], BF16, int(1.5 * KB)),
                      at("p7_bc1", [128, 1024], BF16, int(3.5 * KB))]
        self.p7_tmp = [at("p7_tmpA", [128, 1024], BF16, 8 * KB),
                       at("p7_tmpB", [128, 1024], BF16, 10 * KB)]
        self.biasbuf = A("biasbuf", [128, 128], F32).ap()
        self.rows_a = at("rows_a", [128, 2048], F32, 24 * KB)
        self.rows_b = at("rows_b", [128, 2048], F32, 0)
        self.rowA = self.rows_a[0:1, :]
        self.rowB = self.rows_a[32:33, :]
        self.rowC = self.rows_a[64:65, :]
        self.rowD = self.rows_a[96:97, :]
        self.rowE = self.rows_b[0:1, :]
        self.rowF = self.rows_b[32:33, :]
        self.rowG = self.rows_b[64:65, :]
        self.rowH = self.rows_b[96:97, :]

        Pp = nc.alloc_psum_tensor
        self.pA = Pp("pA", [128, 2048], F32).ap()
        self.pB = Pp("pB", [128, 2048], F32).ap()
        self.pfree = {"pA": None, "pB": None}

    def din(self, name, shape, dtype=F32):
        if name not in self.dram:
            self.dram[name] = self.nc.dram_tensor(
                name, list(shape), dtype, kind="ExternalInput").ap()
        return self.dram[name]

    def dout(self, name, shape, dtype=F32):
        self.dram[name] = self.nc.dram_tensor(
            name, list(shape), dtype, kind="ExternalOutput").ap()
        return self.dram[name]

    def dint(self, name, shape, dtype=F32):
        if name not in self.dram:
            self.dram[name] = self.nc.dram_tensor(
                name, list(shape), dtype, kind="Internal").ap()
        return self.dram[name]

    def ones_row_at(self, p):
        return self.ones_row[p:p + 1, :]

    def bias_col(self, name, k):
        if name not in self.bias:
            self.bias[name] = self.bias_store[:, self._bias_off:self._bias_off + k]
            self._bias_off += k
            assert self._bias_off <= 768, "bias_store overflow"
        return self.bias[name]

    BROW_SLOTS = {"l0_bqv": 0, "l1_rec": 32, "l1_rb4": 64}

    def bias_row(self, name, k, off=0):
        if name not in self.brow:
            p = self.BROW_SLOTS[name]
            self.brow[name] = (self.brow_store[p:p + 1, off:off + k], p)
        return self.brow[name]


def emit_preload_consts(b):
    ctx = b.ctx
    for nm, sb in (("ident_bf", b.ident_bf), ("ident_f32", b.ident_f32),
                   ("ones_col", b.ones_col), ("ones_row", b.ones_row),
                   ("half2", b.half2), ("u16", b.u16), ("u8", b.u8)):
        ctx.dma("y", lambda e, sb=sb, nm=nm: e.dma_start(out=sb, in_=b.cdram[nm]))


def emit_load_bias_col(b, name, dram_ap_flat, n_feat):
    ctx = b.ctx
    k = n_feat // 128
    col = b.bias_col(name, k)
    src = dram_ap_flat.rearrange("(c p) -> c p", p=128)
    j = 0
    tr = getattr(b, "_bias_tr", 0)
    while j < k:
        nb = min(128, k - j)
        stage = b.biasbuf[0:nb, 0:128]
        ld = ctx.dma("y", lambda e, j=j, nb=nb, stage=stage: e.dma_start(
            out=stage, in_=src[j:j + nb, :]), waits=[(ctx.st, tr)])
        pw = b.pfree["pA"]
        tr = ctx.emit("t", lambda e, nb=nb, stage=stage: e.transpose(
            b.pA[:, 0:nb], stage, b.ident_f32[0:nb, 0:nb]),
            waits=[ld] + ([pw] if pw else []), inc=ctx.st)
        cp = ctx.emit("v", lambda e, j=j, nb=nb: e.tensor_copy(
            col[:, j:j + nb], b.pA[:, 0:nb]), waits=[(ctx.st, tr)], inc=ctx.sv)
        b.pfree["pA"] = (ctx.sv, cp)
        b._bias_tr = tr
        j += nb


def emit_load_bias_row(b, name, dram_ap_flat, n_feat):
    ctx = b.ctx
    row, _p = b.bias_row(name, n_feat)
    return ctx.dma("g", lambda e: e.dma_start(
        out=row, in_=dram_ap_flat.rearrange("(o f) -> o f", o=1)))


def emit_outer_ln(b, xsrc_seq, xn_consumed=0):
    ctx = b.ctx
    st, sv, ss, sg = ctx.st, ctx.sv, ctx.ss, ctx.sg
    xb = [b.xb0, b.xb1]
    xsq = [b.xsq0, b.xsq1]
    sum_c = b.colAB[:, 0:16]
    ssq_c = b.colAB[:, 16:32]
    mu_c = b.colAB[:, 32:48]
    rs_c = b.colAB[:, 48:64]
    pcol = [b.pB[:, 0:16], b.pB[:, 512:528]]
    pcol2 = [b.pB[:, 1024:1040], b.pB[:, 1536:1552]]
    xb_free = [0, 0]
    xsq_free = [0, 0]
    acc_done = 0
    pcol_free = [0, 0]
    pcol2_free = [0, 0]
    for c in range(CH):
        i = c % 2
        ld = ctx.dma("g", lambda e, c=c, i=i: e.dma_start(
            out=xb[i], in_=xsrc_seq[c]),
            waits=[(st, xb_free[i])])
        mm = 0
        for j in range(NT):
            w = [ld] if j == 0 else []
            if j == 0:
                w.append((sv, pcol_free[i]))
            mm = ctx.emit("t", lambda e, i=i, j=j: e.matmul(
                pcol[i][:, j:j + 1], xb[i][:, ts(j, 128)], b.ones_col,
                start=True, stop=True), waits=w, inc=(st if j == NT - 1 else None))
        sq = ctx.emit("s", lambda e, i=i: e.activation(
            xsq[i], xb[i], AF.Square), waits=[ld, (st, xsq_free[i])],
            inc=ss)
        mm2 = 0
        for j in range(NT):
            w = [(ss, sq)] if j == 0 else []
            if j == 0:
                w.append((sv, pcol2_free[i]))
            mm2 = ctx.emit("t", lambda e, i=i, j=j: e.matmul(
                pcol2[i][:, j:j + 1], xsq[i][:, ts(j, 128)], b.ones_col,
                start=True, stop=True),
                waits=w, inc=(st if j == NT - 1 else None))
            if j == NT - 1:
                xb_free[i] = mm2
                xsq_free[i] = mm2
        if c == 0:
            a1 = ctx.emit("v", lambda e, i=i: e.tensor_copy(sum_c, pcol[i]),
                          waits=[(st, mm), (sv, xn_consumed)], inc=sv)
            acc_done = ctx.emit("v", lambda e, i=i: e.tensor_copy(
                ssq_c, pcol2[i]), waits=[(st, mm2), (sv, a1)], inc=sv)
        else:
            a1 = ctx.emit("v", lambda e, i=i: e.tensor_tensor(
                sum_c, sum_c, pcol[i], ALU.add),
                waits=[(st, mm), (sv, acc_done)], inc=sv)
            acc_done = ctx.emit("v", lambda e, i=i: e.tensor_tensor(
                ssq_c, ssq_c, pcol2[i], ALU.add),
                waits=[(st, mm2), (sv, a1)], inc=sv)
        pcol_free[i] = a1
        pcol2_free[i] = acc_done
    v1 = ctx.emit("v", lambda e: e.tensor_scalar(
        mu_c, sum_c, 1.0 / D, None, ALU.mult), waits=[(sv, acc_done)], inc=sv)
    s1 = ctx.emit("s", lambda e: e.activation(b.colAB[:, 0:16], mu_c, AF.Square),
                  waits=[(sv, v1)], inc=ss)
    v2 = ctx.emit("v", lambda e: e.scalar_tensor_tensor(
        rs_c, ssq_c, 1.0 / D, b.colAB[:, 0:16], ALU.mult, ALU.subtract),
        waits=[(ss, s1)], inc=sv)
    v3 = ctx.emit("v", lambda e: e.tensor_scalar(
        rs_c, rs_c, EPS, None, ALU.add), waits=[(sv, v2)], inc=sv)
    s2 = ctx.emit("s", lambda e: e.activation(rs_c, rs_c, AF.Ln),
                  waits=[(sv, v3)], inc=ss)
    s3 = ctx.emit("s", lambda e: e.activation(rs_c, rs_c, AF.Exp, scale=-0.5),
                  waits=[(ss, s2)], inc=ss)
    done_bc = emit_col_broadcast(b, mu_c, b.bcmu, [(sv, v1)],
                                 pfree_key="pA")
    done_bc2 = emit_col_broadcast(b, rs_c, b.bcrs, [(ss, s3)],
                                  pfree_key="pA")
    xb2_free = [0, 0]
    sub_done = 0
    done = 0
    for c in range(CH):
        i = c % 2
        ld = ctx.dma("g", lambda e, c=c, i=i: e.dma_start(
            out=xb[i], in_=xsrc_seq[c]),
            waits=[(st, xb_free[i]), (sv, xb2_free[i])])
        w = [ld, (sv, done)]
        if c == 0:
            w += [(ss, done_bc2), (ss, done_bc)]
        sub_done = ctx.emit("v", lambda e, i=i: e.tensor_tensor(
            b.scr32, xb[i], b.bcmu, ALU.subtract), waits=w, inc=sv)
        done = ctx.emit("v", lambda e, c=c: e.tensor_tensor(
            b.xn[:, ts(c, N)], b.scr32, b.bcrs, ALU.mult),
            waits=[(sv, sub_done)], inc=sv)
        xb2_free[i] = done
    return done


def emit_col_broadcast(b, col16, dst_bf, waits, pfree_key="pA", nt=16,
                       dst_psum=None, evac=True):
    ctx = b.ctx
    st, sv, ss = ctx.st, ctx.sv, ctx.ss
    psum = b.pA if dst_psum is None else dst_psum
    pw = b.pfree.get(pfree_key) if dst_psum is None else None
    u = b.u16 if nt == 16 else b.u8
    w = list(waits)
    trw = b.pfree.get("tr_bc")
    if trw:
        w.append(trw)
    tr = ctx.emit("t", lambda e: e.transpose(
        b.pB[0:128, 1792:1920][0:nt, 0:128], col16, b.ident_f32),
        waits=w, inc=st)
    cp = ctx.emit("v", lambda e: e.tensor_copy(
        b.colT[0:nt, :], b.pB[0:128, 1792:1920][0:nt, 0:128]),
        waits=[(st, tr)], inc=sv)
    b.pfree["tr_bc"] = (sv, cp)
    mm = 0
    for c in range(nt):
        w = [(sv, cp)] if c == 0 else []
        if c == 0 and pw:
            w.append(pw)
        mm = ctx.emit("t", lambda e, c=c: e.matmul(
            psum[:, ts(c, 128)], u[0:nt, ts(c, 128)], b.colT[0:nt, :],
            start=True, stop=True), waits=w,
            inc=(st if c == nt - 1 else None))
    ev = ctx.emit("s", lambda e: e.copy(dst_bf, psum[:, 0:nt * 128]),
                  waits=[(st, mm)], inc=ss)
    if dst_psum is None:
        b.pfree[pfree_key] = (ss, ev)
    return ev


def emit_l0_attn_weights(b, wqv_done_req=0):
    ctx = b.ctx
    wqv = b.din("l0_wqv", [D, 2 * D])
    tgts = []
    for c in range(CH):
        tgts.append(ctx.dma("g", lambda e, c=c: e.dma_start(
            out=b.big[:, ts(c, 2048)], in_=wqv[ts(c, 128), :]),
            waits=[(ctx.st, wqv_done_req)] if c == 0 else []))
    return tgts


def emit_l0_attn(b, xn_done, wqv_tgts, sq, s_consumed=0, nt_lim=NT, stop_after=None, parts=('qv','sm','e2v','ms','tr')):
    ctx = b.ctx
    st, sv, ss, sg = ctx.st, ctx.sv, ctx.ss, ctx.sg
    bqv_row, _pb = b.bias_row("l0_bqv", 2048)
    bo_col = b.bias_col("l0_bo", 8)
    QT = lambda c: b.big[:, (CH + c) * 2048:(CH + c + 1) * 2048]
    pq = b.pA[:, 0:1024]
    pv = b.pA[:, 1024:2048]
    ptr = [b.pA[:, 1024 + (cb % 2) * 512 + (cb // 2) * 128:
                1024 + (cb % 2) * 512 + (cb // 2) * 128 + 128]
           for cb in range(8)]
    pM = b.pB[0:64, 0:1024]
    pS = [b.pB[0:64, 1024:1040], b.pB[0:64, 1920:1936]]
    nm = b.colAB[:, 0:1]
    segs = b.colAB[:, 16:32]
    rec = b.colAB[:, 32:48]
    sacc = b.colAB[0:64, 48:64]
    vtile = b.tokC_v
    vt3 = vtile.rearrange("p (h e) -> p h e", e=65)

    pfA = b.pfree.get("pA")
    pfB = b.pfree.get("pB")
    pfT = b.pfree.get("tr_bc")
    ones_set = ctx.emit("v", lambda e: e.memset(vt3[:, :, 64:65], 1.0),
                        waits=[w for w in (pfT,) if w], inc=sv)

    exp_done = 0
    last_trm = 0
    vcp_done = 0
    e2_done = 0
    q_done = 0
    tr_evac = [0, 0]
    sacc_done = 0
    ps_free = [0, 0]
    mm_MS = 0
    seg_done = 0
    rec_done = 0
    nm_done = 0

    for t in range(nt_lim):
        first_w = [(sv, xn_done), (sv, ones_set)]
        if pfA:
            first_w.append(pfA)
            pfA = None
        if t > 0:
            first_w += [(ss, exp_done), (ss, vcp_done)]
            first_w += [(sv, max(tr_evac))]
        qv_mm = 0
        for nsl in range(NSL):
            for c in range(CH):
                w = list(first_w) if (nsl == 0 and c == 0) else []
                if nsl == 0:
                    w.append(wqv_tgts[c])
                ctx.emit("t", lambda e, t=t, c=c, nsl=nsl: e.matmul(
                    b.pA[:, ts(nsl, 512)],
                    b.xn[:, c * N + 128 * t: c * N + 128 * (t + 1)],
                    b.big[:, c * 2048 + 512 * nsl: c * 2048 + 512 * (nsl + 1)],
                    start=(c == 0), stop=False, skip_group_check=True), waits=w)
            qv_mm = ctx.emit("t", lambda e, nsl=nsl: e.matmul(
                b.pA[:, ts(nsl, 512)], b.ones_row[0:1, :],
                bqv_row[0:1, ts(nsl, 512)],
                start=False, stop=True, skip_group_check=True),
                waits=[], inc=st)
        if 'sm' not in parts:
            continue
        nm_done = ctx.emit("v", lambda e: e.reduce_max(
            nm, pq, AX.X, negate=True),
            waits=[(st, qv_mm), (ss, exp_done), (sv, seg_done)], inc=sv)
        exp_done = ctx.emit("s", lambda e: e.activation(
            b.tokA, pq, AF.Exp, bias=nm),
            waits=[(sv, nm_done), (sv, q_done)], inc=ss)
        seg_done = ctx.emit("v", lambda e: e.tensor_reduce(
            segs, b.tokA.rearrange("p (h d) -> p h d", d=64), AX.X, ALU.add),
            waits=[(ss, exp_done), (sv, rec_done)], inc=sv)
        rec_done = ctx.emit("v", lambda e: e.reciprocal(rec, segs),
                            waits=[(sv, seg_done), (sv, q_done)], inc=sv)
        qd = 0
        w0 = [(sv, rec_done), (st, last_trm), (ss, e2_done)]
        for h in range(16):
            w = list(w0) if h == 0 else []
            qd = ctx.emit("v", lambda e, h=h: e.tensor_scalar(
                b.tokD[:, ts(h, 64)], b.tokA[:, ts(h, 64)],
                rec[:, h:h + 1], S_L0, ALU.mult, ALU.mult),
                waits=w, inc=(sv if h == 15 else None))
        q_done = qd
        if 'e2v' not in parts:
            continue
        e2_done = ctx.emit("s", lambda e: e.activation(b.tokB, b.tokD, AF.Exp),
                           waits=[(sv, q_done), (st, mm_MS)], inc=ss)
        vcp_done = ctx.emit("s", lambda e: e.activation(
            vt3[:, :, 0:64],
            pv.rearrange("p (h e) -> p h e", e=64), AF.Copy),
            waits=[(st, qv_mm), (sv, ones_set), (st, mm_MS)], inc=ss)
        if 'ms' not in parts:
            continue
        i = t % 2
        w0 = [(ss, e2_done), (ss, vcp_done)]
        if pfB:
            w0.append(pfB)
            pfB = None
        if t > 1:
            w0.append((sv, ps_free[i]))
        for h in range(16):
            w = list(w0) if h == 0 else []
            ctx.emit("t", lambda e, t=t, h=h: e.matmul(
                pM[:, ts(h, 64)], b.tokB[:, ts(h, 64)], vtile[:, 65 * h:65 * h + 64],
                start=(t == 0 and h in (0, 8)), stop=(t == NT - 1 and h in (7, 15)),
                skip_group_check=True), waits=w)
        for h in range(16):
            hcol = (h % 2) * 8 + h // 2
            mm_MS = ctx.emit("t", lambda e, t=t, h=h, i=i, hcol=hcol: e.matmul(
                pS[i][:, hcol:hcol + 1], b.tokB[:, ts(h, 64)], b.ones_col[:, 0:1],
                start=(h == 0), stop=(h == 15), skip_group_check=True),
                waits=[], inc=(st if h == 15 else None))
        if t == 0:
            sacc_done = ctx.emit("v", lambda e, i=i: e.tensor_copy(
                sacc, pS[i][:, 0:16]), waits=[(st, mm_MS)], inc=sv)
        else:
            sacc_done = ctx.emit("v", lambda e, i=i: e.tensor_tensor(
                sacc, sacc, pS[i][:, 0:16], ALU.add),
                waits=[(st, mm_MS), (sv, sacc_done)], inc=sv)
        ps_free[i] = sacc_done
        if 'tr' not in parts:
            continue
        big3 = b.big.rearrange("p (c x) -> p c x", x=2048)
        for cb in range(CH):
            w = [(sv, q_done)]
            if cb == 0:
                w.append((ss, vcp_done))
                w += [(sv, tr_evac[0]), (sv, tr_evac[1])]
            trm = ctx.emit("t", lambda e, cb=cb: e.transpose(
                ptr[cb], b.tokD[:, ts(cb, 128)], b.ident_f32),
                waits=w, inc=st)
            last_trm = trm
        for bank in range(2):
            tr_evac[bank] = ctx.emit(
                "v", lambda e, bank=bank, t=t: e.tensor_copy(
                    big3[:, CH + bank:2 * CH:2, ts(t, 128)],
                    b.pA[:, 1024 + bank * 512:1536 + bank * 512].rearrange(
                        "p (r x) -> p r x", x=128)),
                waits=[(st, last_trm)], inc=sv)

    if stop_after == "ph1":
        return sacc_done
    sdr = b.dint("l0_sdr", [1024])
    d1 = ctx.dma("y", lambda e: e.dma_start(
        out=sdr.rearrange("(hp d c) -> d hp c", hp=2, d=64),
        in_=sacc.rearrange("d (hp c) -> d hp c", hp=2)),
        waits=[(sv, sacc_done)])
    d2 = ctx.dma("y", lambda e: e.dma_start(
        out=b.c_cols, in_=sdr.rearrange("(pd c) -> pd c", c=8)),
        waits=[d1])
    r1 = ctx.emit("v", lambda e: e.reciprocal(b.c_cols, b.c_cols),
                  waits=[d2], inc=sv)
    r2 = ctx.emit("v", lambda e: e.tensor_scalar(
        b.c_cols, b.c_cols, S_L0, None, ALU.mult), waits=[(sv, r1)], inc=sv)
    qt_scaled = []
    prev = 0
    for c in range(CH):
        w = [(sv, r2)]
        if prev:
            w.append((sv, prev))
        prev = ctx.emit("v", lambda e, c=c: e.tensor_scalar(
            QT(c), QT(c), b.c_cols[:, c:c + 1], None, ALU.mult),
            waits=w, inc=sv)
        qt_scaled.append(prev)
    if stop_after == "scale":
        return prev
    cv1 = ctx.emit("v", lambda e: e.tensor_copy(b.ctx2[0:64, :], pM),
                   waits=[(sv, sacc_done)], inc=sv)
    cv2 = ctx.emit("v", lambda e: e.tensor_copy(
        b.ctx2[64:128, :], b.ctx2[0:64, :]), waits=[(sv, cv1)], inc=sv)
    at_done = [0] * CH
    pb_read = 0
    for c in range(CH):
        mm = 0
        for hh in range(2):
            h = 2 * c + hh
            base = 64 * (h % 2)
            for nsl in range(NSL):
                w = []
                if hh == 0 and nsl == 0:
                    w = [(sv, qt_scaled[c]), (sv, cv2), (ss, at_done[c - 1] if c else 0)]
                ctx.emit("t", lambda e, c=c, h=h, base=base, nsl=nsl, hh=hh: e.matmul(
                    b.pB[base:base + 64, ts(nsl, 512)],
                    b.ctx2[base:base + 64, ts(h, 64)],
                    QT(c)[base:base + 64, ts(nsl, 512)],
                    start=True, stop=True,
                    skip_group_check=True),
                    waits=w, inc=(st if (hh == 1 and nsl == NSL - 1) else None))
        mm = st.n
        at_done[c] = ctx.emit("s", lambda e, c=c: e.copy(QT(c), b.pB),
                              waits=[(st, mm)], inc=ss)
    if stop_after == "einsum":
        return 0
    wo = b.din("L0_attn_Wo", [D, D])
    wo_ld = [0, 0]
    s_done = 0
    for dout in range(CH):
        i = dout % 2
        wr = b.wrol[:, ts(i, 1024)]
        w = [(ss, at_done[-1])] if dout == 0 else []
        if dout >= 2:
            w.append((st, wo_ld[i]))
        ld = ctx.dma("g", lambda e, dout=dout, wr=wr: e.dma_start(
            out=wr.rearrange("p (k f) -> p k f", f=128),
            in_=wo[:, ts(dout, 128)].rearrange(
                "(k p) f -> p k f", p=128)), waits=w)
        mm = 0
        for c in range(CH):
            for nsl in range(NSL):
                w2 = []
                if c == 0 and nsl == 0:
                    w2 = [ld, (sv, s_done if dout else 0)]
                    if dout == 0:
                        w2.append((ss, at_done[-1]))
                        w2.append((sv, max(tr_evac)))
                mm = ctx.emit("t", lambda e, c=c, nsl=nsl, wr=wr: e.matmul(
                    b.pA[:, ts(nsl, 512)], wr[:, ts(c, 128)],
                    QT(c)[:, ts(nsl, 512)], start=(c == 0), stop=(c == CH - 1),
                    skip_group_check=True),
                    waits=w2, inc=(st if (c == CH - 1 and nsl == NSL - 1) else None))
        wo_ld[i] = mm
        s_done = ctx.emit("v", lambda e, dout=dout: e.tensor_scalar(
            b.sb_s[:, ts(dout, N)], b.pA, bo_col[:, dout:dout + 1], None,
            ALU.add),
            waits=[(st, mm), (sv, s_consumed) if dout == 0 else (sv, 0)], inc=sv)
    b.pfree["pA"] = (ctx.sv, s_done)
    b.pfree["pB"] = (ctx.ss, at_done[-1])
    b.pfree["tr_bc"] = None
    return s_done


def emit_l0_moe_weights_pre(b):
    ctx = b.ctx
    rw = b.din("L0_ffn_router_w", [D, E0])
    return ctx.dma("g", lambda e: e.dma_start(
        out=b.rwt.rearrange("p (c e) -> p c e", e=E0),
        in_=rw.rearrange("(c p) e -> p c e", p=128)))


def emit_l0_moe(b, xn_done, g_done_req=0, n_experts=E0):
    ctx = b.ctx
    st, sv, ss, sg = ctx.st, ctx.sv, ctx.ss, ctx.sg
    w1_d = b.din("L0_ffn_w1", [E0, D, F0])
    w2_d = b.din("L0_ffn_w2", [E0, F0, D])
    b1c = b.bias_col("l0_b1", 2 * E0 * F0 // 256 // 1)
    b2c = b.bias_col("l0_b2", E0 * CH)
    hidT = lambda f: b.big[:, f * 1024:(f + 1) * 1024]
    w2T = lambda k: b.big[:, (16 + k) * 1024:(17 + k) * 1024]
    nmc = b.colAB[:, 0:1]
    Zc = b.colAB[:, 1:2]
    rzc = b.colAB[:, 2:3]
    tmp = [b.tokA, b.tokB]
    coefbc = [b.coefbc0, b.coefbc1]
    pAh = [b.pA[:, 0:1024], b.pA[:, 1024:2048]]
    pBh = [b.pB[:, 0:1024], b.pB[:, 1024:2048]]

    g_done = g_done_req
    w2mm_done = 0
    w1_used = [0, 0]
    hid_mm_done = 0
    bc_done = [0, 0]
    last_hev = 0
    tmp_used = [0, 0]
    pfA = b.pfree.get("pA")
    pfB = b.pfree.get("pB")
    ei = 0

    for tb in range(NTB):
        rt_mm = 0
        for s_i in range(8):
            for c in range(CH):
                w = []
                if s_i == 0 and c == 0:
                    w = [(sv, xn_done), (sv, g_done if tb else 0)]
                    if pfA and tb == 0:
                        w.append(pfA)
                    if tb == 1:
                        w.append((ss, last_hev))
                rt_mm = ctx.emit("t", lambda e, s_i=s_i, c=c, tb=tb: e.matmul(
                    b.pA[:, 14 * s_i:14 * (s_i + 1)],
                    b.xn[:, c * N + tb * TB + 128 * s_i:
                         c * N + tb * TB + 128 * (s_i + 1)],
                    b.rwt[:, 14 * c:14 * (c + 1)],
                    start=(c == 0), stop=(c == CH - 1), skip_group_check=True),
                    waits=w, inc=(st if (c == CH - 1) else None))
        vprev = 0
        sprev = 0
        for s_i in range(8):
            lgs = b.lgt[:, 14 * s_i:14 * (s_i + 1)]
            pAs = b.pA[:, 14 * s_i:14 * (s_i + 1)]
            v1 = ctx.emit("v", lambda e, pAs=pAs: e.reduce_max(
                nmc, pAs, AX.X, negate=True),
                waits=[(st, rt_mm), (sv, vprev), (ss, sprev)], inc=sv)
            s1 = ctx.emit("s", lambda e, pAs=pAs, lgs=lgs: e.activation(
                lgs, pAs, AF.Exp, bias=nmc, accum_out=Zc),
                waits=[(sv, v1)], inc=ss)
            sprev = s1
            v2 = ctx.emit("v", lambda e: e.reciprocal(rzc, Zc),
                          waits=[(ss, s1)], inc=sv)
            v3 = ctx.emit("v", lambda e, pAs=pAs, lgs=lgs: e.tensor_scalar(
                lgs, pAs, nmc, 0.0, ALU.add, ALU.is_equal),
                waits=[(sv, v2)], inc=sv)
            vprev = ctx.emit("v", lambda e, s_i=s_i, lgs=lgs: e.tensor_scalar(
                b.coefstore[:, 14 * s_i:14 * (s_i + 1)], lgs, rzc, None,
                ALU.mult), waits=[(sv, v3)], inc=sv)
        router_done = vprev

        for ex in range(n_experts):
            par = ei % 2
            w = [(sv, router_done), (sv, g_done), (ss, bc_done[par])]
            if pfB and ei == 0 and tb == 0:
                w.append(pfB)
            trc = ctx.emit("t", lambda e, ex=ex: e.transpose(
                b.pB[0:8, 0:128], b.coefstore[:, ex:112:14],
                b.ident_f32), waits=w, inc=st)
            cpc = ctx.emit("v", lambda e: e.tensor_copy(
                b.colT[0:8, :], b.pB[0:8, 0:128]), waits=[(st, trc)], inc=sv)
            bcm = 0
            for u in range(8):
                bcm = ctx.emit("t", lambda e, u=u: e.matmul(
                    b.pB[:, 1024 + 128 * u:1024 + 128 * (u + 1)],
                    b.u8[0:8, ts(u, 128)], b.colT[0:8, :],
                    start=True, stop=True, skip_group_check=True),
                    waits=[(sv, cpc)] if u == 0 else [],
                    inc=(st if u == 7 else None))
            bc_done[par] = ctx.emit("s", lambda e, par=par: e.copy(
                coefbc[par], b.pB[:, 1024:2048]), waits=[(st, bcm)], inc=ss)

            w2_tgts = []
            for k in range(16):
                w = [(st, w2mm_done)] if k == 0 else []
                w2_tgts.append(ctx.dma("g", lambda e, ex=ex, k=k: e.dma_start(
                    out=w2T(k), in_=w2_d[ex, ts(k, 128), :]), waits=w))

            hev = 0
            hev_prev2 = [0, 0]
            for f_c in range(16):
                fpar = f_c % 2
                wld = [(st, w1_used[fpar])]
                w1t = b.wrol[:, fpar * 1024:fpar * 1024 + 1024]
                ld = ctx.dma("g", lambda e, ex=ex, f_c=f_c, w1t=w1t: e.dma_start(
                    out=w1t.rearrange("p (k f) -> p k f", f=128),
                    in_=w1_d[ex, :, ts(f_c, 128)].rearrange(
                        "(k p) f -> p k f", p=128)), waits=wld)
                mm = 0
                for c in range(CH):
                    for nsl in range(2):
                        w = []
                        if c == 0 and nsl == 0:
                            w = [ld, (ss, hev_prev2[fpar]),
                                 (st, hid_mm_done if (f_c == 0 and ei > 0) else 0)]
                            if f_c == 0:
                                w.append((sv, router_done))
                        mm = ctx.emit(
                            "t", lambda e, c=c, nsl=nsl, tb=tb, fpar=fpar,
                            w1t=w1t: e.matmul(
                                pAh[fpar][:, ts(nsl, 512)],
                                w1t[:, ts(c, 128)],
                                b.xn[:, c * N + tb * TB + 512 * nsl:
                                     c * N + tb * TB + 512 * (nsl + 1)],
                                start=(c == 0), stop=(c == CH - 1),
                                skip_group_check=True),
                            waits=w,
                            inc=(st if (c == CH - 1 and nsl == 1) else None))
                w1_used[fpar] = mm
                hev = ctx.emit("s", lambda e, ex=ex, f_c=f_c, fpar=fpar: e.activation(
                    hidT(f_c), pAh[fpar], AF.Gelu,
                    bias=b1c[:, ex * 16 + f_c:ex * 16 + f_c + 1]),
                    waits=[(st, mm)], inc=ss)
                hev_prev2[fpar] = hev
                last_hev = hev
            hid_done = hev

            for dout in range(CH):
                dpar = dout % 2
                mm = 0
                for f_c in range(16):
                    for nsl in range(2):
                        w = []
                        if f_c == 0 and nsl == 0:
                            w = [(ss, hid_done), (sv, tmp_used[dpar])]
                            if dout == 0:
                                w.append((ss, bc_done[par]))
                                w += w2_tgts
                        mm = ctx.emit(
                            "t", lambda e, dout=dout, f_c=f_c, nsl=nsl,
                            dpar=dpar: e.matmul(
                                pBh[dpar][:, ts(nsl, 512)],
                                w2T(f_c)[:, ts(dout, 128)],
                                hidT(f_c)[:, ts(nsl, 512)],
                                start=(f_c == 0), stop=(f_c == 15),
                                skip_group_check=True),
                            waits=w,
                            inc=(st if (f_c == 15 and nsl == 1) else None))
                hid_mm_done = mm
                w2mm_done = mm
                gsl = b.sb_g[:, dout * N + tb * TB:dout * N + tb * TB + TB]
                if ex == 0:
                    g_done = ctx.emit(
                        "v", lambda e, dout=dout, dpar=dpar, par=par, ex=ex,
                        gsl=gsl: e.scalar_tensor_tensor(
                            gsl, pBh[dpar],
                            b2c[:, ex * CH + dout:ex * CH + dout + 1],
                            coefbc[par], ALU.add, ALU.mult),
                        waits=[(st, mm), (sv, g_done)], inc=sv)
                    tmp_used[dpar] = g_done
                else:
                    tv = ctx.emit(
                        "v", lambda e, dpar=dpar, par=par, ex=ex, dout=dout:
                        e.scalar_tensor_tensor(
                            tmp[dpar], pBh[dpar],
                            b2c[:, ex * CH + dout:ex * CH + dout + 1],
                            coefbc[par], ALU.add, ALU.mult),
                        waits=[(st, mm), (sv, tmp_used[dpar])], inc=sv)
                    g_done = ctx.emit(
                        "v", lambda e, dpar=dpar, gsl=gsl: e.tensor_tensor(
                            gsl, gsl, tmp[dpar], ALU.add),
                        waits=[(sv, tv), (sv, g_done)], inc=sv)
                    tmp_used[dpar] = g_done
            ei += 1
    b.pfree["pA"] = (sv, router_done)
    b.pfree["pB"] = (sv, g_done)
    return g_done


def emit_combine(b, xsrc_seq, xdst_seq, s_done, g_done, out_f32=True):
    ctx = b.ctx
    st, sv, ss = ctx.st, ctx.sv, ctx.ss
    xb = [b.x40, b.x41]
    used = [(sv, 0), (sv, 0)]
    stores = []
    silu_done = 0
    mul_done = 0
    add_done = 0
    for c in range(CH):
        i = c % 2
        w = [used[i]]
        if c >= 2:
            w.append(stores[c - 2])
        if c < 2:
            w += [(sv, s_done), (sv, g_done)]
        ld = ctx.dma("y", lambda e, c=c, i=i: e.dma_start(
            out=xb[i], in_=xsrc_seq[c]), waits=w)
        silu_done = ctx.emit("s", lambda e, c=c: e.activation(
            b.sg4, b.sb_g[:, ts(c, N)], AF.Silu),
            waits=[(sv, mul_done), (sv, g_done) if c == 0 else (sv, 0)], inc=ss)
        mul_done = ctx.emit("v", lambda e, c=c: e.tensor_tensor(
            b.tmp4, b.sg4, b.sb_s[:, ts(c, N)], ALU.mult),
            waits=[(ss, silu_done), (sv, add_done)], inc=sv)
        add_done = ctx.emit("v", lambda e, i=i: e.tensor_tensor(
            xb[i], xb[i], b.tmp4, ALU.add),
            waits=[(sv, mul_done), ld], inc=sv)
        pr = ctx.dma("y", lambda e, c=c, i=i: e.dma_start(
            out=xdst_seq[c], in_=xb[i]), waits=[(sv, add_done)])
        stores.append(pr)
        used[i] = (sv, add_done)
    return stores


def emit_projA(b, src_fn, dst_fn, w_dram, din_ch, dout_ch, bias, act,
               scale, t0, tlen, first_waits, src_ready, state, evac_fn=None,
               load_gate=None):
    ctx = b.ctx
    st, sv, ss = ctx.st, ctx.sv, ctx.ss
    nsl = (tlen + 511) // 512
    pfree = state.setdefault("pfree", [[], []])
    wused = state.setdefault("wused", [0, 0])
    last = 0
    last_eng = sv
    for dout in range(dout_ch):
        i = dout % 2
        wr = b.wrol[:, i * 2048:i * 2048 + din_ch * 128]
        lw = [(st, wused[i])]
        if dout == 0 and load_gate is not None:
            lw.append(load_gate)
        ld = ctx.dma("g", lambda e, dout=dout, wr=wr: e.dma_start(
            out=wr.rearrange("p (k f) -> p k f", f=128),
            in_=w_dram[:, ts(dout, 128)].rearrange("(k p) f -> p k f", p=128)),
            waits=lw)
        psum = b.pA if i == 0 else b.pB
        mm = 0
        for c in range(din_ch):
            for s_i in range(nsl):
                w = []
                if c == 0 and s_i == 0:
                    w = [ld] + list(pfree[i])
                    if src_ready is not None:
                        w.append(src_ready)
                    if dout == 0:
                        w += list(first_waits)
                mm = ctx.emit("t", lambda e, c=c, s_i=s_i, wr=wr, psum=psum:
                              e.matmul(
                    psum[:, ts(s_i, 512)], wr[:, ts(c, 128)],
                    src_fn(c)[:, s_i * 512:min(tlen, (s_i + 1) * 512)],
                    start=(c == 0), stop=(c == din_ch - 1),
                    skip_group_check=True),
                    waits=w,
                    inc=(st if (c == din_ch - 1 and s_i == nsl - 1) else None))
        wused[i] = mm
        if evac_fn is not None:
            last, last_eng = evac_fn(dout, psum, mm, i)
            pfree[i] = [(last_eng, last)]
            continue
        dst = dst_fn(dout)
        if act == "ts_add":
            last = ctx.emit("v", lambda e, dout=dout, dst=dst, psum=psum:
                            e.tensor_scalar(
                                dst, psum[:, 0:tlen],
                                bias[0][:, bias[1] + dout:bias[1] + dout + 1],
                                None if scale is None else scale,
                                ALU.add,
                                ALU.bypass if scale is None else ALU.mult),
                            waits=[(st, mm)], inc=sv)
            last_eng = sv
            pfree[i] = [(sv, last)]
        else:
            kw = {}
            if bias is not None:
                kw["bias"] = bias[0][:, bias[1] + dout:bias[1] + dout + 1]
            if scale is not None:
                kw["scale"] = scale
            last = ctx.emit("s", lambda e, dout=dout, dst=dst, psum=psum,
                            kw=kw: e.activation(
                                dst, psum[:, 0:tlen],
                                act if act is not None else AF.Copy, **kw),
                            waits=[(st, mm)], inc=ss)
            last_eng = ss
            pfree[i] = [(ss, last)]
    return last, last_eng


def emit_inner_ln(b, buf_fn, dst_fn, g_col, b_col, ready, ready_eng, state_tag):
    ctx = b.ctx
    st, sv, ss = ctx.st, ctx.sv, ctx.ss
    sum_c = b.colAB[:, 0:16]
    ssq_c = b.colAB[:, 16:32]
    mu_c = b.colAB[:, 32:48]
    rs_c = b.colAB[:, 48:64]
    pcol = [b.pB[:, 0:16], b.pB[:, 512:528]]
    pcol2 = [b.pB[:, 1024:1040], b.pB[:, 1536:1552]]
    xsq = [b.xsq0, b.xsq1]
    acc = 0
    pw = b.pfree.get("pB")
    sq_free = [0, 0]
    pcol_free = [0, 0]
    pcol2_free = [0, 0]
    for c in range(CH):
        i = c % 2
        mm = 0
        for j in range(NT):
            w = []
            if j == 0:
                w = [(sv, pcol_free[i])]
                if c == 0:
                    w.append((ready_eng, ready))
                    if pw:
                        w.append(pw)
            mm = ctx.emit("t", lambda e, c=c, j=j, i=i: e.matmul(
                pcol[i][:, j:j + 1], buf_fn(c)[:, ts(j, 128)], b.ones_col,
                start=True, stop=True, skip_group_check=True),
                waits=w, inc=(st if j == NT - 1 else None))
        sq = ctx.emit("s", lambda e, c=c, i=i: e.activation(
            xsq[i], buf_fn(c), AF.Square),
            waits=[(st, sq_free[i]), (ready_eng, ready) if c == 0 else (sv, 0)],
            inc=ss)
        mm2 = 0
        for j in range(NT):
            w = [(ss, sq), (sv, pcol2_free[i])] if j == 0 else []
            mm2 = ctx.emit("t", lambda e, j=j, i=i: e.matmul(
                pcol2[i][:, j:j + 1], xsq[i][:, ts(j, 128)], b.ones_col,
                start=True, stop=True, skip_group_check=True),
                waits=w, inc=(st if j == NT - 1 else None))
        sq_free[i] = mm2
        if c == 0:
            a1 = ctx.emit("v", lambda e, i=i: e.tensor_copy(sum_c, pcol[i]),
                          waits=[(st, mm)], inc=sv)
            acc = ctx.emit("v", lambda e, i=i: e.tensor_copy(ssq_c, pcol2[i]),
                           waits=[(st, mm2), (sv, a1)], inc=sv)
        else:
            a1 = ctx.emit("v", lambda e, i=i: e.tensor_tensor(
                sum_c, sum_c, pcol[i], ALU.add), waits=[(st, mm), (sv, acc)],
                inc=sv)
            acc = ctx.emit("v", lambda e, i=i: e.tensor_tensor(
                ssq_c, ssq_c, pcol2[i], ALU.add),
                waits=[(st, mm2), (sv, a1)], inc=sv)
        pcol_free[i] = a1
        pcol2_free[i] = acc
    v1 = ctx.emit("v", lambda e: e.tensor_scalar(
        mu_c, sum_c, 1.0 / D, None, ALU.mult), waits=[(sv, acc)], inc=sv)
    s1 = ctx.emit("s", lambda e: e.activation(sum_c, mu_c, AF.Square),
                  waits=[(sv, v1)], inc=ss)
    v2 = ctx.emit("v", lambda e: e.scalar_tensor_tensor(
        rs_c, ssq_c, 1.0 / D, sum_c, ALU.mult, ALU.subtract),
        waits=[(ss, s1)], inc=sv)
    v3 = ctx.emit("v", lambda e: e.tensor_scalar(
        rs_c, rs_c, EPS, None, ALU.add), waits=[(sv, v2)], inc=sv)
    s2 = ctx.emit("s", lambda e: e.activation(rs_c, rs_c, AF.Ln),
                  waits=[(sv, v3)], inc=ss)
    s3 = ctx.emit("s", lambda e: e.activation(rs_c, rs_c, AF.Exp, scale=-0.5),
                  waits=[(ss, s2)], inc=ss)
    bc1 = emit_col_broadcast(b, mu_c, b.bcmu, [(sv, v1)])
    bc2 = emit_col_broadcast(b, rs_c, b.bcrs, [(ss, s3)])
    done = 0
    for c in range(CH):
        w = [(ss, bc2), (ss, bc1)] if c == 0 else []
        d1 = ctx.emit("v", lambda e, c=c: e.tensor_tensor(
            b.scr32, buf_fn(c), b.bcmu, ALU.subtract), waits=w + [(sv, done)],
            inc=sv)
        d2 = ctx.emit("v", lambda e, c=c: e.tensor_tensor(
            b.scr32, b.scr32, b.bcrs, ALU.mult), waits=[(sv, d1)], inc=sv)
        done = ctx.emit("v", lambda e, c=c: e.tensor_scalar(
            dst_fn(c), b.scr32, g_col[:, c:c + 1], b_col[:, c:c + 1],
            ALU.mult, ALU.add), waits=[(sv, d2)], inc=sv)
    b.pfree["pB"] = (ctx.sv, pcol2_free[1] if pcol2_free[1] else acc)
    return done


def emit_l1_attn(b, xn_done, s_consumed=0, g_consumed=0, nh_lim=16):
    ctx = b.ctx
    st, sv, ss = ctx.st, ctx.sv, ctx.ss
    wq = b.din("L1_attn_Wq", [D, D])
    wkv = b.din("L1_attn_Wkv", [D, 128])
    wo = b.din("L1_attn_Wo", [D, D])
    g1c = b.bias_col("ln1_g", 8)
    b1c = b.bias_col("ln1_b", 8)
    g2c = b.bias_col("ln2_g", 8)
    b2c = b.bias_col("ln2_b", 8)
    xln = lambda c: b.big[:, ts(c, 2048)]
    qfm = lambda c: b.big[:, (CH + c) * 2048:(CH + c + 1) * 2048]
    ln1 = emit_inner_ln(b, lambda c: b.xn[:, ts(c, N)], xln, g1c, b1c,
                        xn_done, sv, "ln1")
    stq = {}
    pfA = b.pfree.get("pA")
    fw = [pfA] if pfA else []
    qdone, qeng = emit_projA(b, xln, qfm, wq, CH, CH, None, None, SCALE_L1,
                             0, N, fw, (sv, ln1), stq, load_gate=(sv, ln1))
    ldkv = ctx.dma("g", lambda e: e.dma_start(
        out=b.wrol[:, 2048:3072].rearrange("p (k f) -> p k f", f=128),
        in_=wkv.rearrange("(k p) f -> p k f", p=128)),
        waits=[(st, stq["wused"][1])])
    kvmm = 0
    for c in range(CH):
        for s_i in range(NSL):
            w = []
            if c == 0 and s_i == 0:
                w = [ldkv, (qeng, qdone), (st, stq["wused"][0])]
            kvmm = ctx.emit("t", lambda e, c=c, s_i=s_i: e.matmul(
                b.pA[:, ts(s_i, 512)],
                b.wrol[:, 2048 + 128 * c:2048 + 128 * (c + 1)],
                xln(c)[:, ts(s_i, 512)], start=(c == 0), stop=(c == CH - 1),
                skip_group_check=True), waits=w,
                inc=(st if (c == CH - 1 and s_i == NSL - 1) else None))
    k1 = ctx.emit("v", lambda e: e.tensor_copy(b.kk[0:64, :], b.pA[0:64, 0:2048]),
                  waits=[(st, kvmm)], inc=sv)
    k2 = ctx.emit("v", lambda e: e.tensor_copy(b.kk[64:128, :], b.kk[0:64, :]),
                  waits=[(sv, k1)], inc=sv)
    vf = ctx.emit("s", lambda e: e.copy(b.vf32[64:128, :], b.pA[64:128, 0:2048]),
                  waits=[(st, kvmm)], inc=ss)
    trv = 0
    for j in range(NT):
        w = [(ss, vf), (sv, k1)] if j == 0 else []
        if j == 0 and b.pfree.get("pB"):
            w.append(b.pfree["pB"])
        trv = ctx.emit("t", lambda e, j=j: e.transpose(
            b.pB[:, 64 * j:64 * (j + 1)], b.vf32[64:128, ts(j, 128)],
            b.ident_f32[64:128, 64:128]), waits=w, inc=st)
    vcp = ctx.emit("v", lambda e: e.tensor_copy(
        b.vaug.rearrange("p (h e) -> p h e", e=72)[:, :, 0:64],
        b.pB[:, 0:1024].rearrange("p (j x) -> p j x", x=64)),
        waits=[(st, trv)], inc=sv)
    vone = ctx.emit("v", lambda e: e.memset(
        b.vaug.rearrange("p (h e) -> p h e", e=72)[:, :, 64:65], 1.0),
        waits=[(sv, vcp)], inc=sv)
    pt = [b.pt0, b.pt1]
    pAh = [b.pA[:, 0:1024], b.pA[:, 1024:2048]]
    pBh = [b.pB[:, 0:1024], b.pB[:, 1024:2048]]
    exp_done = [0, 0]
    pv_done = [0, 0]
    norm_done = [0, 0]
    rec_row, _ = b.bias_row("l1_rec", 1024)
    oi = 0
    s1_ready = 0
    for h in range(nh_lim):
        hb = 64 * (h % 2)
        for ihalf in range(2):
            op = oi % 2
            io0 = ihalf * 1024
            lastmm = 0
            for j in range(NT):
                jp = j % 2
                w = []
                if j <= 1:
                    w = [(sv, vone), (qeng, qdone), (sv, k2),
                         (sv, norm_done[op]), (ss, exp_done[jp])]
                    if h == 0 and ihalf == 0 and j == 0:
                        w.append((sv, g_consumed))
                else:
                    w.append((ss, exp_done[jp]))
                stmm = 0
                for s_i in range(2):
                    ww = w if s_i == 0 else []
                    stmm = ctx.emit("t", lambda e, h=h, hb=hb, j=j, jp=jp,
                                    s_i=s_i, io0=io0: e.matmul(
                                        pAh[jp][:, ts(s_i, 512)],
                                        b.kk[hb:hb + 64, ts(j, 128)],
                                        qfm(h // 2)[hb:hb + 64, io0 + 512 * s_i:
                                                    io0 + 512 * (s_i + 1)],
                                        start=True, stop=True,
                                        skip_group_check=True),
                                    waits=ww, inc=(st if s_i == 1 else None))
                ed = ctx.emit("s", lambda e, jp=jp: e.activation(
                    pt[jp][:, 0:1024], pAh[jp], AF.Exp),
                    waits=[(st, stmm), (st, pv_done[jp]),
                           (sv, norm_done[0]), (sv, norm_done[1])], inc=ss)
                exp_done[jp] = ed
                for s_i in range(2):
                    w2 = [(ss, ed)] if s_i == 0 else []
                    lastmm = ctx.emit("t", lambda e, j=j, jp=jp, s_i=s_i,
                                      op=op: e.matmul(
                                          pBh[op][0:65, ts(s_i, 512)],
                                          b.vaug[:, 72 * j:72 * j + 65],
                                          pt[jp][:, ts(s_i, 512)],
                                          start=(j == 0), stop=(j == NT - 1),
                                          skip_group_check=True),
                                      waits=w2,
                                      inc=(st if s_i == 1 else None))
                pv_done[jp] = lastmm
            l1 = ctx.emit("s", lambda e, op=op: e.activation(
                b.rows_a[64:65, 0:1024], pBh[op][64:65, :], AF.Ln),
                waits=[(st, lastmm)], inc=ss)
            l2 = ctx.emit("s", lambda e: e.activation(
                b.rows_a[64:65, 0:1024], b.rows_a[64:65, 0:1024], AF.Exp,
                scale=-1.0), waits=[(ss, l1)], inc=ss)
            l3 = ctx.emit("s", lambda e: e.copy(rec_row, b.rows_a[64:65, 0:1024]),
                          waits=[(ss, l2)], inc=ss)
            bcm = 0
            for s_i in range(2):
                w2 = [(ss, l3), (ss, exp_done[0]), (ss, exp_done[1])] \
                    if s_i == 0 else []
                bcm = ctx.emit("t", lambda e, s_i=s_i: e.matmul(
                    b.pA[0:64, ts(s_i, 512)], b.ones_row[32:33, 0:64],
                    rec_row[0:1, ts(s_i, 512)], start=True, stop=True,
                    skip_group_check=True), waits=w2,
                    inc=(st if s_i == 1 else None))
            rb = ctx.emit("s", lambda e: e.copy(b.recbc[0:64, 0:1024],
                                                b.pA[0:64, 0:1024]),
                          waits=[(st, bcm)], inc=ss)
            norm_done[op] = ctx.emit(
                "v", lambda e, h=h, hb=hb, io0=io0, op=op: e.tensor_tensor(
                    b.sb_g[hb:hb + 64, (h // 2) * 2048 + io0:
                           (h // 2) * 2048 + io0 + 1024],
                    pBh[op][0:64, :], b.recbc[0:64, 0:1024], ALU.mult),
                waits=[(ss, rb)], inc=sv)
            exp_done[0] = max(exp_done[0], 0)
            s1_ready = norm_done[op]
            oi += 1
            exp_done[0] = rb if rb > exp_done[0] else exp_done[0]
            exp_done[1] = rb if rb > exp_done[1] else exp_done[1]
    attnT_done = s1_ready
    stw = {"wused": stq["wused"]}
    wodone, woeng = emit_projA(
        b, lambda c: b.sb_g[:, ts(c, 2048)], lambda c: b.sb_s[:, ts(c, N)],
        wo, CH, CH, None, None, None, 0, N,
        [(sv, attnT_done), (sv, s_consumed)], (sv, attnT_done), stw,
        load_gate=(sv, vone))
    ln2 = emit_inner_ln(b, lambda c: b.sb_s[:, ts(c, N)],
                        lambda c: b.sb_s[:, ts(c, N)], g2c, b2c,
                        wodone, woeng, "ln2")
    b.pfree["pA"] = (sv, ln2)
    b.pfree["pB"] = (sv, ln2)
    b.l1_wused = stw["wused"]
    return ln2


def emit_coef_bc8(b, src_strided, dst_bf, waits):
    ctx = b.ctx
    st, sv, ss = ctx.st, ctx.sv, ctx.ss
    trc = ctx.emit("t", lambda e: e.transpose(
        b.pB[0:8, 0:128], src_strided, b.ident_f32), waits=waits, inc=st)
    cpc = ctx.emit("v", lambda e: e.tensor_copy(
        b.colT[0:8, :], b.pB[0:8, 0:128]), waits=[(st, trc)], inc=sv)
    bcm = 0
    for u in range(8):
        bcm = ctx.emit("t", lambda e, u=u: e.matmul(
            b.pB[:, 1024 + 128 * u:1024 + 128 * (u + 1)],
            b.u8[0:8, ts(u, 128)], b.colT[0:8, :], start=True, stop=True,
            skip_group_check=True),
            waits=[(sv, cpc)] if u == 0 else [], inc=(st if u == 7 else None))
    ev = ctx.emit("s", lambda e, dst_bf=dst_bf: e.copy(
        dst_bf, b.pB[:, 1024:2048]), waits=[(st, bcm)], inc=ss)
    return ev


def emit_l1_ffn(b, xn_done, g_gate=0):
    ctx = b.ctx
    st, sv, ss, sg = ctx.st, ctx.sv, ctx.ss, ctx.sg
    bigA = lambda c: b.big[:, ts(c, 2048)]
    bigB = lambda c: b.big[:, (CH + c) * 2048:(CH + c + 1) * 2048]
    bigBh = lambda f: b.big[:, 16384 + f * 1024:16384 + (f + 1) * 1024]
    bigAh = lambda f: b.big[:, f * 1024:(f + 1) * 1024]
    rw1 = b.din("L1_ffn_router_w", [D, 4])
    rb_d = b.din("L1_ffn_router_b", [4])
    lb_d = b.din("L1_ffn_load_balancer", [4])
    sw1 = b.din("L1_ffn_sw_w1", [D, I1])
    sw3 = b.din("L1_ffn_sw_w3", [D, I1])
    sw2 = b.din("L1_ffn_sw_w2", [I1, D])
    eqw = b.din("L1_ffn_me_eq_w", [D, D])
    mewv = b.din("L1_ffn_me_wv", [D, D])
    mewo = b.din("L1_ffn_me_wo", [D, D])
    c1w = b.din("L1_ffn_me_c1w", [D, 2 * D])
    c2w = b.din("L1_ffn_me_c2w", [2 * D, D])
    synw = b.din("L1_ffn_ce_syn_w", [D, D])
    cewv = b.din("L1_ffn_ce_wv", [D, D])
    cewo = b.din("L1_ffn_ce_wo", [D, D])
    f1w = b.din("L1_ffn_ce_f1w", [D, 2 * D])
    f2w = b.din("L1_ffn_ce_f2w", [2 * D, D])
    genw = b.din("L1_ffn_ce_gen_w", [D, D])
    mlw1 = b.din("L1_ffn_ml_w1", [D, I1])
    mlw2 = b.din("L1_ffn_ml_w2", [I1, D])
    eq_b = b.bias_col("me_eq_b", 8)
    me_bv = b.bias_col("me_bv", 8)
    me_bo = b.bias_col("me_bo", 8)
    c1b = b.bias_col("me_c1b", 16)
    c2b = b.bias_col("me_c2b", 8)
    syn_b = b.bias_col("ce_syn_b", 8)
    ce_bv = b.bias_col("ce_bv", 8)
    ce_bo = b.bias_col("ce_bo", 8)
    n1g = b.bias_col("ce_n1g", 8)
    n1b = b.bias_col("ce_n1b", 8)
    f1b = b.bias_col("ce_f1b", 16)
    f2b = b.bias_col("ce_f2b", 8)
    n2g = b.bias_col("ce_n2g", 8)
    n2b = b.bias_col("ce_n2b", 8)
    gen_b = b.bias_col("ce_gen_b", 8)
    ml_b1 = b.bias_col("ml_b1", 32)
    ml_b2 = b.bias_col("ml_b2", 8)
    rb_row, _ = b.bias_row("l1_rb4", 4, off=2000)

    d1 = ctx.dma("y", lambda e: e.dma_start(
        out=b.rows_a[0:1, 0:4], in_=rb_d.rearrange("(o f) -> o f", o=1)))
    d2 = ctx.dma("y", lambda e: e.dma_start(
        out=b.rows_a[0:1, 4:8], in_=lb_d.rearrange("(o f) -> o f", o=1)))
    v1 = ctx.emit("v", lambda e: e.tensor_tensor(
        b.rows_a[0:1, 8:12], b.rows_a[0:1, 0:4], b.rows_a[0:1, 4:8], ALU.add),
        waits=[d1, d2], inc=sv)
    s1 = ctx.emit("s", lambda e: e.copy(rb_row, b.rows_a[0:1, 8:12]),
                  waits=[(sv, v1)], inc=ss)

    pfA = b.pfree.get("pA")
    rt_mm = 0
    for s_i in range(NT):
        for c in range(CH):
            w = []
            if s_i == 0 and c == 0:
                w = [(sv, xn_done), (ss, s1)]
                if pfA:
                    w.append(pfA)
            ctx.emit("t", lambda e, s_i=s_i, c=c: e.matmul(
                b.pA[:, 4 * s_i:4 * (s_i + 1)],
                b.xn[:, c * N + 128 * s_i:c * N + 128 * (s_i + 1)],
                b.rwt1[:, 4 * c:4 * (c + 1)],
                start=(c == 0), stop=False, skip_group_check=True), waits=w)
        rt_mm = ctx.emit("t", lambda e, s_i=s_i: e.matmul(
            b.pA[:, 4 * s_i:4 * (s_i + 1)], b.ones_row[64:65, :],
            rb_row, start=False, stop=True, skip_group_check=True),
            waits=[], inc=st)
    vprev = ctx.emit("v", lambda e: e.memset(b.p7_coefstore, 0.0),
                     waits=[(sv, g_gate)], inc=sv)
    sprev = 0
    for s_i in range(NT):
        L = b.p7_lgt[:, 16 * s_i:16 * (s_i + 1)]
        q = b.pA[:, 4 * s_i:4 * (s_i + 1)]

        def col(k, L=L):
            return L[:, k:k + 1]
        v0 = ctx.emit("v", lambda e, L=L, q=q: e.tensor_copy(L[:, 0:4], q),
                      waits=[(st, rt_mm), (sv, vprev), (ss, sprev)], inc=sv)
        ops = [
            lambda e, L=L: e.tensor_tensor(col(4, L), col(0, L), col(1, L), ALU.max),
            lambda e, L=L: e.tensor_tensor(col(5, L), col(0, L), col(1, L), ALU.min),
            lambda e, L=L: e.tensor_tensor(col(6, L), col(2, L), col(3, L), ALU.max),
            lambda e, L=L: e.tensor_tensor(col(7, L), col(2, L), col(3, L), ALU.min),
            lambda e, L=L: e.tensor_tensor(col(8, L), col(4, L), col(6, L), ALU.max),
            lambda e, L=L: e.tensor_tensor(col(9, L), col(4, L), col(6, L), ALU.min),
            lambda e, L=L: e.tensor_tensor(col(10, L), col(5, L), col(7, L), ALU.max),
            lambda e, L=L: e.tensor_tensor(col(12, L), col(9, L), col(10, L), ALU.max),
        ]
        vv = v0
        for f in ops:
            vv = ctx.emit("v", f, waits=[(sv, vv)], inc=sv)
        vv = ctx.emit("v", lambda e, L=L: e.tensor_tensor(
            col(13, L), col(12, L), col(8, L), ALU.subtract),
            waits=[(sv, vv)], inc=sv)
        sx = ctx.emit("s", lambda e, L=L: e.activation(
            col(13, L), col(13, L), AF.Exp), waits=[(sv, vv)], inc=ss)
        vv = ctx.emit("v", lambda e, L=L: e.tensor_scalar(
            col(13, L), col(13, L), 1.0, None, ALU.add),
            waits=[(ss, sx)], inc=sv)
        vv = ctx.emit("v", lambda e, L=L: e.reciprocal(col(14, L), col(13, L)),
                      waits=[(sv, vv)], inc=sv)
        for ex in range(4):
            vv = ctx.emit("v", lambda e, L=L, ex=ex: e.tensor_tensor(
                col(15, L), col(ex, L), col(8, L), ALU.subtract),
                waits=[(sv, vv)], inc=sv)
            sx = ctx.emit("s", lambda e, L=L: e.activation(
                col(15, L), col(15, L), AF.Exp), waits=[(sv, vv)], inc=ss)
            vv = ctx.emit("v", lambda e, L=L, ex=ex, s_i=s_i: e.tensor_tensor(
                b.p7_coefstore[:, 4 * s_i + ex:4 * s_i + ex + 1],
                col(ex, L), col(12, L), ALU.is_ge), waits=[(ss, sx)], inc=sv)
            vv = ctx.emit("v", lambda e, L=L, ex=ex, s_i=s_i:
                          e.scalar_tensor_tensor(
                              b.p7_coefstore[:, 4 * s_i + ex:4 * s_i + ex + 1],
                              col(15, L), col(14, L),
                              b.p7_coefstore[:, 4 * s_i + ex:4 * s_i + ex + 1],
                              ALU.mult, ALU.mult), waits=[(sv, vv)], inc=sv)
            sprev = sx
        vprev = vv
    router_done = vprev

    g_done = g_gate
    bc_done = [0, 0]
    tmp_used = [0, 0]
    ei = 0

    def g_update(dout, psum, mm, wpar, ex, tb, bias_col, bias_base, first):
        nonlocal g_done, ei
        ct = coef_tile[0]
        gsl = b.sb_g[:, dout * N + tb * TB:dout * N + tb * TB + TB]
        if first:
            tgt = ctx.emit("v", lambda e, dout=dout, psum=psum, ct=ct: (
                e.scalar_tensor_tensor(
                    gsl, psum[:, 0:TB],
                    bias_col[:, bias_base + dout:bias_base + dout + 1],
                    ct, ALU.add, ALU.mult)
                if bias_col is not None else
                e.tensor_tensor(gsl, psum[:, 0:TB], ct, ALU.mult)),
                waits=[(st, mm), (sv, g_done)], inc=sv)
            g_done = tgt
            return tgt, sv
        tpar = dout % 2
        tv = ctx.emit("v", lambda e, dout=dout, psum=psum, tpar=tpar, ct=ct: (
            e.scalar_tensor_tensor(
                b.p7_tmp[tpar], psum[:, 0:TB],
                bias_col[:, bias_base + dout:bias_base + dout + 1],
                ct, ALU.add, ALU.mult)
            if bias_col is not None else
            e.tensor_tensor(b.p7_tmp[tpar], psum[:, 0:TB], ct, ALU.mult)),
            waits=[(st, mm), (sv, tmp_used[tpar])], inc=sv)
        tgt = ctx.emit("v", lambda e, tpar=tpar, gsl=gsl: e.tensor_tensor(
            gsl, gsl, b.p7_tmp[tpar], ALU.add),
            waits=[(sv, tv), (sv, g_done)], inc=sv)
        tmp_used[tpar] = tgt
        g_done = tgt
        return tgt, sv

    coef_tile = [None]

    stA = {}
    psums4 = [b.pA[:, 0:1024], b.pA[:, 1024:2048],
              b.pB[:, 0:1024], b.pB[:, 1024:2048]]
    amm = 0
    sil_done = 0
    mul_done = 0
    hev2 = [0, 0]
    wused4 = [0, 0, 0, 0]
    p2_evac = 0
    for ex, (wa, wb, w2d, hb1, hb1b, hb2, hb2b, swiglu) in enumerate([
            (sw1, sw3, sw2, None, 0, None, 0, True),
            (mlw1, None, mlw2, ml_b1, 0, ml_b2, 0, False)]):
        exn = 0 if swiglu else 3
        for tb in range(NTB):
            par = ei % 2
            bc_w = [(sv, router_done), (ss, bc_done[par]), (sv, g_done),
                    (st, p2_evac if False else 0)]
            bc_done[par] = emit_coef_bc8(
                b, b.p7_coefstore[:, 32 * tb + exn:32 * tb + 32:4],
                b.p7_bc[par], bc_w)
            coef_tile[0] = b.p7_bc[par]
            for f_c in range(32):
                i = f_c % 2
                wsl = b.wrol[:, i * 1024:(i + 1) * 1024]
                wsl3 = b.wrol[:, 2048 + i * 1024:2048 + (i + 1) * 1024]
                lw = [(st, wused4[i])]
                ld1 = ctx.dma("g", lambda e, f_c=f_c, wsl=wsl, wa=wa: e.dma_start(
                    out=wsl.rearrange("p (k f) -> p k f", f=128),
                    in_=wa[:, ts(f_c, 128)].rearrange("(k p) f -> p k f", p=128)),
                    waits=lw)
                if swiglu:
                    ld3 = ctx.dma("g", lambda e, f_c=f_c, wsl3=wsl3, wb=wb:
                                  e.dma_start(
                                      out=wsl3.rearrange("p (k f) -> p k f", f=128),
                                      in_=wb[:, ts(f_c, 128)].rearrange(
                                          "(k p) f -> p k f", p=128)),
                                  waits=[(st, wused4[2 + i])])
                mm = mm3 = 0
                for c in range(CH):
                    for s_i in range(2):
                        w = []
                        if c == 0 and s_i == 0:
                            w = [ld1, (ss, sil_done if swiglu else hev2[0]),
                                 (ss, bc_done[par])]
                            if not swiglu:
                                w = [ld1, (ss, hev2[i]), (ss, bc_done[par])]
                        mm = ctx.emit("t", lambda e, c=c, s_i=s_i, tb=tb,
                                      wsl=wsl, i=i, swiglu=swiglu: e.matmul(
                                          (psums4[0] if swiglu else psums4[i])
                                          [:, ts(s_i, 512)],
                                          wsl[:, ts(c, 128)],
                                          b.xn[:, c * N + tb * TB + 512 * s_i:
                                               c * N + tb * TB + 512 * (s_i + 1)],
                                          start=(c == 0), stop=(c == CH - 1),
                                          skip_group_check=True),
                                      waits=w,
                                      inc=(st if c == CH - 1 and s_i == 1 else None))
                wused4[i] = mm
                if swiglu:
                    for c in range(CH):
                        for s_i in range(2):
                            w = [ld3, (sv, mul_done)] if (c == 0 and s_i == 0) else []
                            mm3 = ctx.emit("t", lambda e, c=c, s_i=s_i, tb=tb,
                                           wsl3=wsl3: e.matmul(
                                               psums4[1][:, ts(s_i, 512)],
                                               wsl3[:, ts(c, 128)],
                                               b.xn[:, c * N + tb * TB + 512 * s_i:
                                                    c * N + tb * TB + 512 * (s_i + 1)],
                                               start=(c == 0), stop=(c == CH - 1),
                                               skip_group_check=True),
                                           waits=w,
                                           inc=(st if c == CH - 1 and s_i == 1 else None))
                    wused4[2 + i] = mm3
                    sil_done = ctx.emit("s", lambda e: e.activation(
                        b.p7_tmp[0], psums4[0], AF.Silu),
                        waits=[(st, mm), (sv, mul_done)], inc=ss)
                    mul_done = ctx.emit("v", lambda e, f_c=f_c: e.tensor_tensor(
                        bigAh(f_c), b.p7_tmp[0], psums4[1],
                        ALU.mult), waits=[(ss, sil_done), (st, mm3)], inc=sv)
                else:
                    hev2[i] = ctx.emit("s", lambda e, f_c=f_c, i=i: e.activation(
                        bigAh(f_c), psums4[i], AF.Gelu,
                        bias=hb1[:, f_c:f_c + 1]),
                        waits=[(st, mm)], inc=ss)
            hid_ready = (sv, mul_done) if swiglu else (ss, hev2[(32 - 1) % 2])
            if swiglu:
                tmp_used[0] = max(tmp_used[0], mul_done)
            for grp in range(2):
                for f_c in range(32):
                    i = f_c % 2
                    wsl = b.wrol[:, i * 1024:(i + 1) * 1024]
                    lw = [(st, wused4[i])]
                    if grp == 0 and f_c < 2:
                        lw.append(hid_ready)
                    ld = ctx.dma("g", lambda e, f_c=f_c, wsl=wsl, w2d=w2d:
                                 e.dma_start(
                                     out=wsl, in_=w2d[ts(f_c, 128), :]),
                                 waits=lw)
                    mm = 0
                    for dg in range(4):
                        for s_i in range(2):
                            w = []
                            if dg == 0 and s_i == 0:
                                w = [ld, hid_ready]
                                if f_c == 0:
                                    w.append((sv, g_done))
                                    w.append((sv, tmp_used[0]))
                                    w.append((sv, tmp_used[1]))
                                    w.append((ss, bc_done[par]))
                            mm = ctx.emit("t", lambda e, grp=grp, dg=dg,
                                          s_i=s_i, f_c=f_c, wsl=wsl: e.matmul(
                                              psums4[dg][:, ts(s_i, 512)],
                                              wsl[:, 128 * (grp * 4 + dg):
                                                  128 * (grp * 4 + dg + 1)],
                                              bigAh(f_c)[:, ts(s_i, 512)],
                                              start=(f_c == 0), stop=(f_c == 31),
                                              skip_group_check=True),
                                          waits=w,
                                          inc=(st if (dg == 3 and s_i == 1) else None))
                    wused4[i] = mm
                for dg in range(4):
                    t1, _e = g_update(grp * 4 + dg, psums4[dg], mm, None,
                                      exn, tb, hb2, 0, ei < 2)
                    p2_evac = t1
            ei += 1
    e03_done = g_done

    stP = {"wused": [wused4[0], wused4[1]],
           "pfree": [[(sv, g_done)], [(sv, g_done)]]}

    def chain_proj(src_fn, dst_fn, w, bias, act, gate, ready, ready_eng,
                   tlen=N, din=CH, dout=CH, evac_fn=None):
        stP_local = stP
        done, eng = emit_projA(b, src_fn, dst_fn, w, din, dout, bias, act,
                               None, 0, tlen, [(ready_eng, ready)],
                               (ready_eng, ready),
                               stP_local, evac_fn=evac_fn, load_gate=gate)
        return done, eng

    def coef_bc(exn, tb, extra_waits):
        par = ei % 2
        w = [(sv, router_done), (ss, bc_done[par])] + extra_waits
        if "pfree" in stP:
            w += list(stP["pfree"][0]) + list(stP["pfree"][1])
        tgt = emit_coef_bc8(
            b, b.p7_coefstore[:, 32 * tb + exn:32 * tb + 32:4],
            b.p7_bc[par], w)
        bc_done[par] = tgt
        if "pfree" in stP:
            stP["pfree"][0] = list(stP["pfree"][0]) + [(ss, tgt)]
            stP["pfree"][1] = list(stP["pfree"][1]) + [(ss, tgt)]
        return par, tgt

    r = (sv, e03_done)
    eqd, eqe = chain_proj(lambda c: b.xn[:, ts(c, N)], bigB, eqw,
                          (eq_b, 0), "ts_add", r, e03_done, sv)
    t1d, t1e = chain_proj(bigB, bigA, mewv, (me_bv, 0), "ts_add",
                          (eqe, eqd), eqd, eqe)
    syd, sye = chain_proj(bigA, bigB, mewo, (me_bo, 0), "ts_add",
                          (t1e, t1d), t1d, t1e)
    for tb in range(NTB):
        par, bct = coef_bc(1, tb, [(sv, g_done)])
        coef_tile[0] = b.p7_bc[par]
        h1d, h1e = chain_proj(
            lambda c, tb=tb: bigB(c)[:, tb * TB:(tb + 1) * TB],
            lambda f: bigAh(f), c1w, (c1b, 0), AF.Gelu,
            (sye, syd), syd, sye, tlen=TB, din=CH, dout=16)

        def evac_e1(dout, psum, mm, wpar, tb=tb, par=par):
            return g_update(dout, psum, mm, wpar, 1, tb, c2b, 0, False)
        g1d, g1e = chain_proj(
            lambda k: bigAh(k), None, c2w, None, None,
            (h1e, h1d), h1d, h1e, tlen=TB, din=16, dout=CH, evac_fn=evac_e1)
        ei += 1

    snd, sne = chain_proj(lambda c: b.xn[:, ts(c, N)], bigB, synw,
                          (syn_b, 0), "ts_add", (sv, g_done), g_done, sv)
    u1d, u1e = chain_proj(bigB, bigA, cewv, (ce_bv, 0), "ts_add",
                          (sne, snd), snd, sne)

    def evac_add_syn(dout, psum, mm, wpar):
        tgt = ctx.emit("v", lambda e, dout=dout, psum=psum:
                       e.scalar_tensor_tensor(
                           bigB(dout), psum, ce_bo[:, dout:dout + 1],
                           bigB(dout), ALU.add, ALU.add),
                       waits=[(st, mm)], inc=sv)
        return tgt, sv
    h2d, h2e = chain_proj(bigA, None, cewo, None, None, (u1e, u1d), u1d, u1e,
                          evac_fn=evac_add_syn)
    ln1d = emit_inner_ln(b, bigB, bigB, n1g, n1b, h2d, h2e, "e2ln1")
    for tb in range(NTB):
        f1d, f1e = chain_proj(
            lambda c, tb=tb: bigB(c)[:, tb * TB:(tb + 1) * TB],
            lambda f: bigAh(f), f1w, (f1b, 0), AF.Relu,
            (sv, ln1d), ln1d, sv, tlen=TB, din=CH, dout=16)

        def evac_add_h2(dout, psum, mm, wpar, tb=tb):
            tgt = ctx.emit("v", lambda e, dout=dout, psum=psum, tb=tb:
                           e.scalar_tensor_tensor(
                               bigB(dout)[:, tb * TB:(tb + 1) * TB],
                               psum[:, 0:TB], f2b[:, dout:dout + 1],
                               bigB(dout)[:, tb * TB:(tb + 1) * TB],
                               ALU.add, ALU.add),
                           waits=[(st, mm)], inc=sv)
            return tgt, sv
        f2d, f2e = chain_proj(
            lambda k: bigAh(k), None, f2w, None, None,
            (f1e, f1d), f1d, f1e, tlen=TB, din=16, dout=CH,
            evac_fn=evac_add_h2)
        ln1d = f2d
    ln2d = emit_inner_ln(b, bigB, bigB, n2g, n2b, ln1d, sv, "e2ln2")
    for tb in range(NTB):
        par, bct = coef_bc(2, tb, [(sv, g_done), (sv, ln2d)])
        coef_tile[0] = b.p7_bc[par]

        def evac_e2(dout, psum, mm, wpar, tb=tb):
            return g_update(dout, psum, mm, wpar, 2, tb, gen_b, 0, False)
        g2d, g2e = chain_proj(
            lambda c, tb=tb: bigB(c)[:, tb * TB:(tb + 1) * TB],
            None, genw, None, None, (sv, ln2d), ln2d, sv,
            tlen=TB, din=CH, dout=CH, evac_fn=evac_e2)
        ei += 1

    b.pfree["pA"] = (sv, g_done)
    b.pfree["pB"] = (sv, g_done)
    return g_done


def emit_store_fm(b, sbuf_ap, dram_seq, done_sem, done_val):
    ctx = b.ctx
    prs = []
    for c in range(CH):
        prs.append(ctx.dma("y", lambda e, c=c: e.dma_start(
            out=dram_seq[c], in_=sbuf_ap[:, ts(c, N)]),
            waits=[(done_sem, done_val)] if c == 0 else []))
    return prs


def emit_load_fm(b, sbuf_ap, dram_seq, waits=()):
    ctx = b.ctx
    prs = []
    for c in range(CH):
        prs.append(ctx.dma("g", lambda e, c=c: e.dma_start(
            out=sbuf_ap[:, ts(c, N)], in_=dram_seq[c]),
            waits=waits if c == 0 else []))
    return prs


def emit_final_drain(b):
    ctx = b.ctx
    waits = tuple((s.h, s.n) for s in ctx.all_sems() if s.n > 0)
    ctx.q["g"].append((None, waits, None, 0))
    ctx.q["g"].append((lambda e: e.nop(), (), None, 0))


def finalize_and_run(b, in_maps, core_ids=None, **kw):
    with b.nc.Block() as block:
        b.ctx.finalize(block)
    if core_ids is None:
        core_ids = list(range(len(in_maps)))
    return run_bass_kernel_spmd(b.nc, in_maps, core_ids=core_ids, **kw)


def emit_all_preloads(b):
    ctx = b.ctx
    emit_preload_consts(b)
    b.din("l0_wqv", [D, 2 * D])
    bqv = b.din("l0_bqv", [2 * D])
    emit_load_bias_row(b, "l0_bqv", bqv, 2048)
    emit_load_bias_col(b, "l0_bo", b.din("L0_attn_bo", [D]), D)
    emit_load_bias_col(b, "l0_b1", b.din("L0_ffn_b1", [E0, F0]).rearrange(
        "a b -> (a b)"), E0 * F0)
    emit_load_bias_col(b, "l0_b2", b.din("L0_ffn_b2", [E0, D]).rearrange(
        "a b -> (a b)"), E0 * D)
    emit_l0_moe_weights_pre(b)
    emit_load_bias_col(b, "ln1_g", b.din("L1_attn_ln1_g", [D]), D)
    emit_load_bias_col(b, "ln1_b", b.din("L1_attn_ln1_b", [D]), D)
    emit_load_bias_col(b, "ln2_g", b.din("L1_attn_ln2_g", [D]), D)
    emit_load_bias_col(b, "ln2_b", b.din("L1_attn_ln2_b", [D]), D)
    for cname, dname, nf in [
            ("me_eq_b", "L1_ffn_me_eq_b", D), ("me_bv", "L1_ffn_me_bv", D),
            ("me_bo", "L1_ffn_me_bo", D), ("me_c1b", "L1_ffn_me_c1b", 2 * D),
            ("me_c2b", "L1_ffn_me_c2b", D),
            ("ce_syn_b", "L1_ffn_ce_syn_b", D), ("ce_bv", "L1_ffn_ce_bv", D),
            ("ce_bo", "L1_ffn_ce_bo", D), ("ce_n1g", "L1_ffn_ce_n1g", D),
            ("ce_n1b", "L1_ffn_ce_n1b", D), ("ce_f1b", "L1_ffn_ce_f1b", 2 * D),
            ("ce_f2b", "L1_ffn_ce_f2b", D), ("ce_n2g", "L1_ffn_ce_n2g", D),
            ("ce_n2b", "L1_ffn_ce_n2b", D), ("ce_gen_b", "L1_ffn_ce_gen_b", D),
            ("ml_b1", "L1_ffn_ml_b1", I1), ("ml_b2", "L1_ffn_ml_b2", D)]:
        emit_load_bias_col(b, cname, b.din(dname, [nf]), nf)
    rw1 = b.din("L1_ffn_router_w", [D, 4])
    ctx.dma("g", lambda e: e.dma_start(
        out=b.rwt1.rearrange("p (c e) -> p c e", e=4),
        in_=rw1.rearrange("(c p) e -> p c e", p=128)))


def build_full(nseq=SEQS):
    b = B()
    ctx = b.ctx
    x0 = b.din("x0", [SEQS, CH, 128, N])
    y = b.dout("y", [SEQS, CH, 128, N])
    x1h = b.dint("x1h", [SEQS, CH, 128, N])
    emit_all_preloads(b)
    ctx.barrier()
    for sq in range(nseq):
        wqv_tgts = emit_l0_attn_weights(b)
        xn_done = emit_outer_ln(b, x0[sq])
        s0_done = emit_l0_attn(b, xn_done, wqv_tgts, None)
        ctx.barrier()
        g0_done = emit_l0_moe(b, ctx.sv.n)
        ctx.barrier()
        emit_combine(b, x0[sq], x1h[sq], ctx.sv.n, ctx.sv.n)
        ctx.barrier()
        xn1_done = emit_outer_ln(b, x1h[sq])
        s1_done = emit_l1_attn(b, xn1_done)
        ctx.barrier()
        g1_done = emit_l1_ffn(b, ctx.sv.n)
        ctx.barrier()
        emit_combine(b, x1h[sq], y[sq], ctx.sv.n, ctx.sv.n)
        ctx.barrier()
    emit_final_drain(b)
    return b


def kernel(**inputs):
    x = np.asarray(inputs["L0_attn_x"], dtype=np.float32)
    wqv = np.ascontiguousarray(np.concatenate(
        [inputs["L0_attn_Wq"], inputs["L0_attn_Wv"]], axis=1).astype(np.float32))
    bqv = np.ascontiguousarray(np.concatenate(
        [inputs["L0_attn_bq"], inputs["L0_attn_bv"]]).astype(np.float32))
    shared = {"l0_wqv": wqv, "l0_bqv": bqv}
    for nm in ["L0_attn_Wo", "L0_attn_bo", "L0_ffn_router_w", "L0_ffn_w1",
               "L0_ffn_b1", "L0_ffn_w2", "L0_ffn_b2",
               "L1_attn_ln1_g", "L1_attn_ln1_b", "L1_attn_Wq", "L1_attn_Wkv",
               "L1_attn_Wo", "L1_attn_ln2_g", "L1_attn_ln2_b",
               "L1_ffn_router_w", "L1_ffn_router_b", "L1_ffn_load_balancer",
               "L1_ffn_sw_w1", "L1_ffn_sw_w3", "L1_ffn_sw_w2",
               "L1_ffn_me_eq_w", "L1_ffn_me_eq_b", "L1_ffn_me_wv",
               "L1_ffn_me_bv", "L1_ffn_me_wo", "L1_ffn_me_bo",
               "L1_ffn_me_c1w", "L1_ffn_me_c1b", "L1_ffn_me_c2w",
               "L1_ffn_me_c2b", "L1_ffn_ce_syn_w", "L1_ffn_ce_syn_b",
               "L1_ffn_ce_wv", "L1_ffn_ce_bv", "L1_ffn_ce_wo",
               "L1_ffn_ce_bo", "L1_ffn_ce_n1g", "L1_ffn_ce_n1b",
               "L1_ffn_ce_f1w", "L1_ffn_ce_f1b", "L1_ffn_ce_f2w",
               "L1_ffn_ce_f2b", "L1_ffn_ce_n2g", "L1_ffn_ce_n2b",
               "L1_ffn_ce_gen_w", "L1_ffn_ce_gen_b", "L1_ffn_ml_w1",
               "L1_ffn_ml_b1", "L1_ffn_ml_w2", "L1_ffn_ml_b2"]:
        shared[nm] = np.ascontiguousarray(np.asarray(inputs[nm], np.float32))
    in_maps = []
    for c in range(NCORES):
        xc = x[SEQS * c:SEQS * (c + 1)]
        xf = np.ascontiguousarray(xc.transpose(0, 2, 1)).reshape(
            SEQS, CH, 128, N)
        m = dict(shared)
        m["x0"] = xf
        in_maps.append(m)
    b = build_full()
    with b.nc.Block() as block:
        b.ctx.finalize(block)
    res = run_bass_kernel_spmd(b.nc, in_maps, core_ids=list(range(NCORES)))
    outs = []
    for c in range(NCORES):
        yc = np.asarray(res.results[c]["y"]).reshape(SEQS, D, N)
        outs.append(yc.transpose(0, 2, 1))
    return np.ascontiguousarray(np.concatenate(outs, axis=0))


# version 02
# speedup vs baseline: 1.0994x; 1.0994x over previous
import numpy as np
import ml_dtypes
import concourse.bass as bass
import concourse.mybir as mybir
from concourse.bass_utils import run_bass_kernel_spmd

F32 = mybir.dt.float32
BF16 = mybir.dt.bfloat16
AF = mybir.ActivationFunctionType
ALU = mybir.AluOpType
AX = mybir.AxisListType
BF = ml_dtypes.bfloat16

NCORES = 8
SEQS = 4
N = 2048
D = 1024
CH = D // 128
NT = N // 128
NSL = N // 512
S_L0 = float((64.0 ** 0.5) ** -0.25)
SCALE_L1 = float(64.0 ** -0.5)
EPS = 1e-5
E0, F0 = 14, 2048
E1, I1 = 4, 4096
TB = 1024
NTB = N // TB


class Sem:
    __slots__ = ("h", "n", "name")

    def __init__(self, nc, name):
        self.h = nc.alloc_semaphore(name)
        self.n = 0
        self.name = name


class Ctx:

    def __init__(self, nc):
        self.nc = nc
        self.q = {e: [] for e in ("t", "v", "s", "g", "y")}
        self.st = Sem(nc, "st")
        self.sv = Sem(nc, "sv")
        self.ss = Sem(nc, "ss")
        self.sg = Sem(nc, "sg")
        self.sy = Sem(nc, "sy")
        self.glanes = [Sem(nc, f"gl{i}") for i in range(8)]
        self.ylanes = [Sem(nc, f"yl{i}") for i in range(8)]
        self._gi = 0
        self._yi = 0

    def emit(self, eng, build, waits=(), inc=None, inck=1):
        tgt = None
        if inc is not None:
            inc.n += inck
            tgt = inc.n
        self.q[eng].append((build, tuple((s.h, v) for s, v in waits if v > 0),
                            (inc.h if inc else None), inck))
        return tgt

    def dma(self, eng, build, waits=()):
        if eng == "g":
            lanes = self.glanes
            self._gi += 1
            lane = lanes[self._gi % len(lanes)]
        else:
            lanes = self.ylanes
            self._yi += 1
            lane = lanes[self._yi % len(lanes)]
        w = list(waits)
        if lane.n:
            w.append((lane, lane.n))
        tgt = self.emit(eng, build, waits=w, inc=lane, inck=16)
        return (lane, tgt)

    def all_sems(self):
        return [self.st, self.sv, self.ss, self.sg, self.sy] + \
            self.glanes + self.ylanes

    def barrier(self, engines="tvsgy"):
        sems = self.all_sems()
        snap = [(s, s.n) for s in sems if s.n > 0]
        for e in engines:
            own = {"t": self.st, "v": self.sv, "s": self.ss}.get(e)
            waits = tuple((s.h, v) for s, v in snap if s is not own)
            if waits:
                self.q[e].append((None, waits, None, 0))

    def finalize(self, block):
        def drain(engine, lst):
            for build, waits, inch, inck in lst:
                for h, v in waits:
                    engine.wait_ge(h, v)
                if build is not None:
                    ins = build(engine)
                    if inch is not None:
                        ins.then_inc(inch, inck)

        for key, meth in (("t", block.tensor), ("v", block.vector),
                          ("s", block.scalar), ("g", block.gpsimd),
                          ("y", block.sync)):
            lst = self.q[key]
            if lst:
                meth(lambda e, lst=lst: drain(e, lst))


def ts(i, k):
    return slice(i * k, (i + 1) * k)


def make_const_data():
    c = {}
    c["ident_bf"] = np.eye(128, dtype=BF)
    c["ident_f32"] = np.eye(128, dtype=np.float32)
    c["ones_col"] = np.ones((128, 1), dtype=BF)
    c["ones_row"] = np.ones((128, 128), dtype=BF)
    h2 = np.zeros((2, 128), dtype=BF)
    h2[0, 0:64] = 1
    h2[1, 64:128] = 1
    c["half2"] = h2
    u16 = np.zeros((16, 16 * 128), dtype=BF)
    for e in range(16):
        u16[e, ts(e, 128)] = 1
    c["u16"] = u16
    u8 = np.zeros((8, 8 * 128), dtype=BF)
    for e in range(8):
        u8[e, ts(e, 128)] = 1
    c["u8"] = u8
    return c


class B:

    def __init__(self):
        self.nc = bass.Bass("TRN2", target_bir_lowering=False)
        nc = self.nc
        self.ctx = Ctx(nc)
        self.dram = {}
        self.wlist = []
        self.cdata = make_const_data()
        self.cdram = {k: nc.inline_tensor(v, "c_" + k).ap()
                      for k, v in self.cdata.items()}

        A = nc.alloc_sbuf_tensor
        self.xn = A("xn", [128, CH * N], BF16).ap()
        self.sb_s = A("sb_s", [128, CH * N], BF16).ap()
        self.sb_g = A("sb_g", [128, CH * N], BF16).ap()
        self.big = A("big", [128, 2 * CH * N], BF16).ap()
        self.ident_bf = A("t_ident_bf", [128, 128], BF16).ap()
        self.ident_f32 = A("t_ident_f32", [128, 128], F32).ap()
        self.ones_col = A("t_ones_col", [128, 1], BF16).ap()
        self.ones_row = A("t_ones_row", [128, 128], BF16).ap()
        self.half2 = A("t_half2", [2, 128], BF16).ap()
        self.u16 = A("t_u16", [16, 16 * 128], BF16).ap()
        self.u8 = A("t_u8", [8, 8 * 128], BF16).ap()
        self.colAB = A("colAB", [128, 64], F32).ap()
        self.colT = A("colT", [16, 128], BF16).ap()
        self.rwt = A("rwt", [128, 14 * 8], BF16).ap()
        self.rwt1 = A("rwt1", [128, 4 * 8], BF16).ap()
        self.bias_store = A("bias_store", [128, 768], F32).ap()
        self._bias_off = 0
        self.bias = {}
        self.brow_store = A("brow_store", [128, 2048], BF16).ap()
        self._brow_slot = 0
        self.brow = {}
        r0, _ = nc.bump_sbuf(28 * 1024)
        self.Roff = r0

        def at(name, shape, dtype, off_b):
            return nc.alloc_sbuf_tensor_at(
                name, shape, dtype, offset=r0 + off_b).ap()

        KB = 1024
        self.xb0 = at("xb0", [128, 2048], BF16, 0)
        self.xb1 = at("xb1", [128, 2048], BF16, 4 * KB)
        self.xsq0 = at("xsq0", [128, 2048], BF16, 8 * KB)
        self.xsq1 = at("xsq1", [128, 2048], BF16, 12 * KB)
        self.bcmu = at("bcmu", [128, 2048], BF16, 8 * KB)
        self.bcrs = at("bcrs", [128, 2048], BF16, 12 * KB)
        self.scr32 = at("scr32", [128, 2048], F32, 16 * KB)
        self.tokA = at("tokA", [128, 1024], BF16, 0)
        self.tokB = at("tokB", [128, 1024], BF16, 2 * KB)
        self.tokC_v = at("tokC_v", [128, 1040], BF16, 4 * KB)
        self.tokD = at("tokD", [128, 1024], F32, int(6.25 * KB))
        self.ctx2 = at("ctx2", [128, 1024], BF16, int(10.25 * KB))
        self.c_cols = at("c_cols", [128, 8], F32, int(12.5 * KB))
        self.wrol = at("wrol", [128, 4096], BF16, 16 * KB)
        self.coefstore = at("coefstore", [128, 128], F32, 20 * KB)
        self.coefbc0 = at("coefbc0", [128, 1024], BF16, 8 * KB)
        self.coefbc1 = at("coefbc1", [128, 1024], BF16, 10 * KB)
        self.lgt = at("lgt", [128, 256], F32, 12 * KB)
        self.lgt2 = at("lgt2", [128, 256], F32, 13 * KB)
        self.kk = at("kk", [128, 2048], BF16, 0)
        self.vaug = at("vaug", [128, 16 * 72], BF16, 4 * KB)
        self.pt0 = at("pt0", [128, 2048], BF16, 8 * KB)
        self.pt1 = at("pt1", [128, 2048], BF16, 12 * KB)
        self.recbc = at("recbc", [128, 2048], BF16, 8 * KB)
        self.x40 = at("x40", [128, 2048], F32, 0)
        self.x41 = at("x41", [128, 2048], F32, 8 * KB)
        self.sg4 = at("sg4", [128, 2048], BF16, 16 * KB)
        self.tmp4 = at("tmp4", [128, 2048], BF16, 20 * KB)
        self.vf32 = at("vf32", [128, 2048], F32, 8 * KB)
        self.p7_coefstore = at("p7_coefstore", [128, 128], F32, 0)
        self.p7_lgt = at("p7_lgt", [128, 256], F32, 512)
        self.p7_bc = [at("p7_bc0", [128, 1024], BF16, int(1.5 * KB)),
                      at("p7_bc1", [128, 1024], BF16, int(3.5 * KB))]
        self.p7_tmp = [at("p7_tmpA", [128, 1024], BF16, 8 * KB),
                       at("p7_tmpB", [128, 1024], BF16, 10 * KB)]
        self.biasbuf = A("biasbuf", [128, 128], F32).ap()
        self.rows_a = at("rows_a", [128, 2048], F32, 24 * KB)
        self.rows_b = at("rows_b", [128, 2048], F32, 0)
        self.rowA = self.rows_a[0:1, :]
        self.rowB = self.rows_a[32:33, :]
        self.rowC = self.rows_a[64:65, :]
        self.rowD = self.rows_a[96:97, :]
        self.rowE = self.rows_b[0:1, :]
        self.rowF = self.rows_b[32:33, :]
        self.rowG = self.rows_b[64:65, :]
        self.rowH = self.rows_b[96:97, :]

        Pp = nc.alloc_psum_tensor
        self.pA = Pp("pA", [128, 2048], F32).ap()
        self.pB = Pp("pB", [128, 2048], F32).ap()
        self.pfree = {"pA": None, "pB": None}

    def wsrc(self, name, shape):
        key = "ws_" + name
        if key not in self.dram:
            self.din(name, shape)
            self.dram[key] = self.nc.dram_tensor(
                key, list(shape), BF16, kind="Internal").ap()
            self.wlist.append((name, shape))
        return self.dram[key]

    def din(self, name, shape, dtype=F32):
        if name not in self.dram:
            self.dram[name] = self.nc.dram_tensor(
                name, list(shape), dtype, kind="ExternalInput").ap()
        return self.dram[name]

    def dout(self, name, shape, dtype=F32):
        self.dram[name] = self.nc.dram_tensor(
            name, list(shape), dtype, kind="ExternalOutput").ap()
        return self.dram[name]

    def dint(self, name, shape, dtype=F32):
        if name not in self.dram:
            self.dram[name] = self.nc.dram_tensor(
                name, list(shape), dtype, kind="Internal").ap()
        return self.dram[name]

    def ones_row_at(self, p):
        return self.ones_row[p:p + 1, :]

    def bias_col(self, name, k):
        if name not in self.bias:
            self.bias[name] = self.bias_store[:, self._bias_off:self._bias_off + k]
            self._bias_off += k
            assert self._bias_off <= 768, "bias_store overflow"
        return self.bias[name]

    BROW_SLOTS = {"l0_bqv": 0, "l1_rec": 32, "l1_rb4": 64}

    def bias_row(self, name, k, off=0):
        if name not in self.brow:
            p = self.BROW_SLOTS[name]
            self.brow[name] = (self.brow_store[p:p + 1, off:off + k], p)
        return self.brow[name]


def emit_preload_consts(b):
    ctx = b.ctx
    for nm, sb in (("ident_bf", b.ident_bf), ("ident_f32", b.ident_f32),
                   ("ones_col", b.ones_col), ("ones_row", b.ones_row),
                   ("half2", b.half2), ("u16", b.u16), ("u8", b.u8)):
        ctx.dma("y", lambda e, sb=sb, nm=nm: e.dma_start(out=sb, in_=b.cdram[nm]))


def emit_load_bias_col(b, name, dram_ap_flat, n_feat):
    ctx = b.ctx
    k = n_feat // 128
    col = b.bias_col(name, k)
    src = dram_ap_flat.rearrange("(c p) -> c p", p=128)
    j = 0
    tr = getattr(b, "_bias_tr", 0)
    while j < k:
        nb = min(128, k - j)
        stage = b.biasbuf[0:nb, 0:128]
        ld = ctx.dma("y", lambda e, j=j, nb=nb, stage=stage: e.dma_start(
            out=stage, in_=src[j:j + nb, :]), waits=[(ctx.st, tr)])
        pw = b.pfree["pA"]
        tr = ctx.emit("t", lambda e, nb=nb, stage=stage: e.transpose(
            b.pA[:, 0:nb], stage, b.ident_f32[0:nb, 0:nb]),
            waits=[ld] + ([pw] if pw else []), inc=ctx.st)
        cp = ctx.emit("v", lambda e, j=j, nb=nb: e.tensor_copy(
            col[:, j:j + nb], b.pA[:, 0:nb]), waits=[(ctx.st, tr)], inc=ctx.sv)
        b.pfree["pA"] = (ctx.sv, cp)
        b._bias_tr = tr
        j += nb


def emit_load_bias_row(b, name, dram_ap_flat, n_feat):
    ctx = b.ctx
    row, _p = b.bias_row(name, n_feat)
    return ctx.dma("g", lambda e: e.dma_start(
        out=row, in_=dram_ap_flat.rearrange("(o f) -> o f", o=1)))


def emit_outer_ln(b, xsrc_seq, xn_consumed=0):
    ctx = b.ctx
    st, sv, ss, sg = ctx.st, ctx.sv, ctx.ss, ctx.sg
    xb = [b.xb0, b.xb1]
    xsq = [b.xsq0, b.xsq1]
    sum_c = b.colAB[:, 0:16]
    ssq_c = b.colAB[:, 16:32]
    mu_c = b.colAB[:, 32:48]
    rs_c = b.colAB[:, 48:64]
    pcol = [b.pB[:, 0:16], b.pB[:, 512:528]]
    pcol2 = [b.pB[:, 1024:1040], b.pB[:, 1536:1552]]
    xb_free = [0, 0]
    xsq_free = [0, 0]
    acc_done = 0
    pcol_free = [0, 0]
    pcol2_free = [0, 0]
    for c in range(CH):
        i = c % 2
        ld = ctx.dma("g", lambda e, c=c, i=i: e.dma_start(
            out=xb[i], in_=xsrc_seq[c]),
            waits=[(st, xb_free[i])])
        mm = 0
        for j in range(NT):
            w = [ld] if j == 0 else []
            if j == 0:
                w.append((sv, pcol_free[i]))
            mm = ctx.emit("t", lambda e, i=i, j=j: e.matmul(
                pcol[i][:, j:j + 1], xb[i][:, ts(j, 128)], b.ones_col,
                start=True, stop=True), waits=w, inc=(st if j == NT - 1 else None))
        sq = ctx.emit("s", lambda e, i=i: e.activation(
            xsq[i], xb[i], AF.Square), waits=[ld, (st, xsq_free[i])],
            inc=ss)
        mm2 = 0
        for j in range(NT):
            w = [(ss, sq)] if j == 0 else []
            if j == 0:
                w.append((sv, pcol2_free[i]))
            mm2 = ctx.emit("t", lambda e, i=i, j=j: e.matmul(
                pcol2[i][:, j:j + 1], xsq[i][:, ts(j, 128)], b.ones_col,
                start=True, stop=True),
                waits=w, inc=(st if j == NT - 1 else None))
            if j == NT - 1:
                xb_free[i] = mm2
                xsq_free[i] = mm2
        if c == 0:
            a1 = ctx.emit("v", lambda e, i=i: e.tensor_copy(sum_c, pcol[i]),
                          waits=[(st, mm), (sv, xn_consumed)], inc=sv)
            acc_done = ctx.emit("v", lambda e, i=i: e.tensor_copy(
                ssq_c, pcol2[i]), waits=[(st, mm2), (sv, a1)], inc=sv)
        else:
            a1 = ctx.emit("v", lambda e, i=i: e.tensor_tensor(
                sum_c, sum_c, pcol[i], ALU.add),
                waits=[(st, mm), (sv, acc_done)], inc=sv)
            acc_done = ctx.emit("v", lambda e, i=i: e.tensor_tensor(
                ssq_c, ssq_c, pcol2[i], ALU.add),
                waits=[(st, mm2), (sv, a1)], inc=sv)
        pcol_free[i] = a1
        pcol2_free[i] = acc_done
    v1 = ctx.emit("v", lambda e: e.tensor_scalar(
        mu_c, sum_c, 1.0 / D, None, ALU.mult), waits=[(sv, acc_done)], inc=sv)
    s1 = ctx.emit("s", lambda e: e.activation(b.colAB[:, 0:16], mu_c, AF.Square),
                  waits=[(sv, v1)], inc=ss)
    v2 = ctx.emit("v", lambda e: e.scalar_tensor_tensor(
        rs_c, ssq_c, 1.0 / D, b.colAB[:, 0:16], ALU.mult, ALU.subtract),
        waits=[(ss, s1)], inc=sv)
    v3 = ctx.emit("v", lambda e: e.tensor_scalar(
        rs_c, rs_c, EPS, None, ALU.add), waits=[(sv, v2)], inc=sv)
    s2 = ctx.emit("s", lambda e: e.activation(rs_c, rs_c, AF.Ln),
                  waits=[(sv, v3)], inc=ss)
    s3 = ctx.emit("s", lambda e: e.activation(rs_c, rs_c, AF.Exp, scale=-0.5),
                  waits=[(ss, s2)], inc=ss)
    done_bc = emit_col_broadcast(b, mu_c, b.bcmu, [(sv, v1)],
                                 pfree_key="pA")
    done_bc2 = emit_col_broadcast(b, rs_c, b.bcrs, [(ss, s3)],
                                  pfree_key="pA")
    xb2_free = [0, 0]
    sub_done = 0
    done = 0
    for c in range(CH):
        i = c % 2
        ld = ctx.dma("g", lambda e, c=c, i=i: e.dma_start(
            out=xb[i], in_=xsrc_seq[c]),
            waits=[(st, xb_free[i]), (sv, xb2_free[i])])
        w = [ld, (sv, done)]
        if c == 0:
            w += [(ss, done_bc2), (ss, done_bc)]
        sub_done = ctx.emit("v", lambda e, i=i: e.tensor_tensor(
            b.scr32, xb[i], b.bcmu, ALU.subtract), waits=w, inc=sv)
        done = ctx.emit("v", lambda e, c=c: e.tensor_tensor(
            b.xn[:, ts(c, N)], b.scr32, b.bcrs, ALU.mult),
            waits=[(sv, sub_done)], inc=sv)
        xb2_free[i] = done
    return done


def emit_col_broadcast(b, col16, dst_bf, waits, pfree_key="pA", nt=16,
                       dst_psum=None, evac=True):
    ctx = b.ctx
    st, sv, ss = ctx.st, ctx.sv, ctx.ss
    psum = b.pA if dst_psum is None else dst_psum
    pw = b.pfree.get(pfree_key) if dst_psum is None else None
    u = b.u16 if nt == 16 else b.u8
    w = list(waits)
    trw = b.pfree.get("tr_bc")
    if trw:
        w.append(trw)
    tr = ctx.emit("t", lambda e: e.transpose(
        b.pB[0:128, 1792:1920][0:nt, 0:128], col16, b.ident_f32),
        waits=w, inc=st)
    cp = ctx.emit("v", lambda e: e.tensor_copy(
        b.colT[0:nt, :], b.pB[0:128, 1792:1920][0:nt, 0:128]),
        waits=[(st, tr)], inc=sv)
    b.pfree["tr_bc"] = (sv, cp)
    mm = 0
    for c in range(nt):
        w = [(sv, cp)] if c == 0 else []
        if c == 0 and pw:
            w.append(pw)
        mm = ctx.emit("t", lambda e, c=c: e.matmul(
            psum[:, ts(c, 128)], u[0:nt, ts(c, 128)], b.colT[0:nt, :],
            start=True, stop=True), waits=w,
            inc=(st if c == nt - 1 else None))
    ev = ctx.emit("s", lambda e: e.copy(dst_bf, psum[:, 0:nt * 128]),
                  waits=[(st, mm)], inc=ss)
    if dst_psum is None:
        b.pfree[pfree_key] = (ss, ev)
    return ev


def emit_l0_attn_weights(b, wqv_done_req=0):
    ctx = b.ctx
    wqv = b.wsrc("l0_wqv", [D, 2 * D])
    tgts = []
    for c in range(CH):
        tgts.append(ctx.dma("y", lambda e, c=c: e.dma_start(
            out=b.big[:, ts(c, 2048)], in_=wqv[ts(c, 128), :]),
            waits=[(ctx.st, wqv_done_req)] if c == 0 else []))
    return tgts


def emit_l0_attn(b, xn_done, wqv_tgts, sq, s_consumed=0, nt_lim=NT, stop_after=None, parts=('qv','sm','e2v','ms','tr')):
    ctx = b.ctx
    st, sv, ss, sg = ctx.st, ctx.sv, ctx.ss, ctx.sg
    bqv_row, _pb = b.bias_row("l0_bqv", 2048)
    bo_col = b.bias_col("l0_bo", 8)
    QT = lambda c: b.big[:, (CH + c) * 2048:(CH + c + 1) * 2048]
    pq = b.pA[:, 0:1024]
    pv = b.pA[:, 1024:2048]
    ptr = [b.pA[:, 1024 + (cb % 2) * 512 + (cb // 2) * 128:
                1024 + (cb % 2) * 512 + (cb // 2) * 128 + 128]
           for cb in range(8)]
    pM = b.pB[0:64, 0:1024]
    pS = [b.pB[0:64, 1024:1040], b.pB[0:64, 1920:1936]]
    nm = b.colAB[:, 0:1]
    segs = b.colAB[:, 16:32]
    rec = b.colAB[:, 32:48]
    sacc = b.colAB[0:64, 48:64]
    vtile = b.tokC_v
    vt3 = vtile.rearrange("p (h e) -> p h e", e=65)

    pfA = b.pfree.get("pA")
    pfB = b.pfree.get("pB")
    pfT = b.pfree.get("tr_bc")
    ones_set = ctx.emit("v", lambda e: e.memset(vt3[:, :, 64:65], 1.0),
                        waits=[w for w in (pfT,) if w], inc=sv)

    exp_done = 0
    last_trm = 0
    vcp_done = 0
    e2_done = 0
    q_done = 0
    tr_evac = [0, 0]
    sacc_done = 0
    ps_free = [0, 0]
    mm_MS = 0
    seg_done = 0
    rec_done = 0
    nm_done = 0

    for t in range(nt_lim):
        first_w = [(sv, xn_done), (sv, ones_set)]
        if pfA:
            first_w.append(pfA)
            pfA = None
        if t > 0:
            first_w += [(ss, exp_done), (ss, vcp_done)]
            first_w += [(sv, max(tr_evac))]
        qv_mm = 0
        for nsl in range(NSL):
            for c in range(CH):
                w = list(first_w) if (nsl == 0 and c == 0) else []
                if nsl == 0:
                    w.append(wqv_tgts[c])
                ctx.emit("t", lambda e, t=t, c=c, nsl=nsl: e.matmul(
                    b.pA[:, ts(nsl, 512)],
                    b.xn[:, c * N + 128 * t: c * N + 128 * (t + 1)],
                    b.big[:, c * 2048 + 512 * nsl: c * 2048 + 512 * (nsl + 1)],
                    start=(c == 0), stop=False, skip_group_check=True), waits=w)
            qv_mm = ctx.emit("t", lambda e, nsl=nsl: e.matmul(
                b.pA[:, ts(nsl, 512)], b.ones_row[0:1, :],
                bqv_row[0:1, ts(nsl, 512)],
                start=False, stop=True, skip_group_check=True),
                waits=[], inc=st)
        if 'sm' not in parts:
            continue
        nm_done = ctx.emit("v", lambda e: e.reduce_max(
            nm, pq, AX.X, negate=True),
            waits=[(st, qv_mm), (ss, exp_done), (sv, seg_done)], inc=sv)
        exp_done = ctx.emit("s", lambda e: e.activation(
            b.tokA, pq, AF.Exp, bias=nm),
            waits=[(sv, nm_done), (sv, q_done)], inc=ss)
        seg_done = ctx.emit("v", lambda e: e.tensor_reduce(
            segs, b.tokA.rearrange("p (h d) -> p h d", d=64), AX.X, ALU.add),
            waits=[(ss, exp_done), (sv, rec_done)], inc=sv)
        rec_done = ctx.emit("v", lambda e: e.reciprocal(rec, segs),
                            waits=[(sv, seg_done), (sv, q_done)], inc=sv)
        qd = 0
        w0 = [(sv, rec_done), (st, last_trm), (ss, e2_done)]
        for h in range(16):
            w = list(w0) if h == 0 else []
            qd = ctx.emit("v", lambda e, h=h: e.tensor_scalar(
                b.tokD[:, ts(h, 64)], b.tokA[:, ts(h, 64)],
                rec[:, h:h + 1], S_L0, ALU.mult, ALU.mult),
                waits=w, inc=(sv if h == 15 else None))
        q_done = qd
        if 'e2v' not in parts:
            continue
        e2_done = ctx.emit("s", lambda e: e.activation(b.tokB, b.tokD, AF.Exp),
                           waits=[(sv, q_done), (st, mm_MS)], inc=ss)
        vcp_done = ctx.emit("s", lambda e: e.activation(
            vt3[:, :, 0:64],
            pv.rearrange("p (h e) -> p h e", e=64), AF.Copy),
            waits=[(st, qv_mm), (sv, ones_set), (st, mm_MS)], inc=ss)
        if 'ms' not in parts:
            continue
        i = t % 2
        w0 = [(ss, e2_done), (ss, vcp_done)]
        if pfB:
            w0.append(pfB)
            pfB = None
        if t > 1:
            w0.append((sv, ps_free[i]))
        for h in range(16):
            w = list(w0) if h == 0 else []
            ctx.emit("t", lambda e, t=t, h=h: e.matmul(
                pM[:, ts(h, 64)], b.tokB[:, ts(h, 64)], vtile[:, 65 * h:65 * h + 64],
                start=(t == 0 and h in (0, 8)), stop=(t == NT - 1 and h in (7, 15)),
                skip_group_check=True), waits=w)
        for h in range(16):
            hcol = (h % 2) * 8 + h // 2
            mm_MS = ctx.emit("t", lambda e, t=t, h=h, i=i, hcol=hcol: e.matmul(
                pS[i][:, hcol:hcol + 1], b.tokB[:, ts(h, 64)], b.ones_col[:, 0:1],
                start=(h == 0), stop=(h == 15), skip_group_check=True),
                waits=[], inc=(st if h == 15 else None))
        if t == 0:
            sacc_done = ctx.emit("v", lambda e, i=i: e.tensor_copy(
                sacc, pS[i][:, 0:16]), waits=[(st, mm_MS)], inc=sv)
        else:
            sacc_done = ctx.emit("v", lambda e, i=i: e.tensor_tensor(
                sacc, sacc, pS[i][:, 0:16], ALU.add),
                waits=[(st, mm_MS), (sv, sacc_done)], inc=sv)
        ps_free[i] = sacc_done
        if 'tr' not in parts:
            continue
        big3 = b.big.rearrange("p (c x) -> p c x", x=2048)
        for cb in range(CH):
            w = [(sv, q_done)]
            if cb == 0:
                w.append((ss, vcp_done))
                w += [(sv, tr_evac[0]), (sv, tr_evac[1])]
            trm = ctx.emit("t", lambda e, cb=cb: e.transpose(
                ptr[cb], b.tokD[:, ts(cb, 128)], b.ident_f32),
                waits=w, inc=st)
            last_trm = trm
        for bank in range(2):
            tr_evac[bank] = ctx.emit(
                "v", lambda e, bank=bank, t=t: e.tensor_copy(
                    big3[:, CH + bank:2 * CH:2, ts(t, 128)],
                    b.pA[:, 1024 + bank * 512:1536 + bank * 512].rearrange(
                        "p (r x) -> p r x", x=128)),
                waits=[(st, last_trm)], inc=sv)

    if stop_after == "ph1":
        return sacc_done
    sdr = b.dint("l0_sdr", [1024])
    d1 = ctx.dma("y", lambda e: e.dma_start(
        out=sdr.rearrange("(hp d c) -> d hp c", hp=2, d=64),
        in_=sacc.rearrange("d (hp c) -> d hp c", hp=2)),
        waits=[(sv, sacc_done)])
    d2 = ctx.dma("y", lambda e: e.dma_start(
        out=b.c_cols, in_=sdr.rearrange("(pd c) -> pd c", c=8)),
        waits=[d1])
    r1 = ctx.emit("v", lambda e: e.reciprocal(b.c_cols, b.c_cols),
                  waits=[d2], inc=sv)
    r2 = ctx.emit("v", lambda e: e.tensor_scalar(
        b.c_cols, b.c_cols, S_L0, None, ALU.mult), waits=[(sv, r1)], inc=sv)
    qt_scaled = []
    prev = 0
    for c in range(CH):
        w = [(sv, r2)]
        if prev:
            w.append((sv, prev))
        prev = ctx.emit("v", lambda e, c=c: e.tensor_scalar(
            QT(c), QT(c), b.c_cols[:, c:c + 1], None, ALU.mult),
            waits=w, inc=sv)
        qt_scaled.append(prev)
    if stop_after == "scale":
        return prev
    cv1 = ctx.emit("v", lambda e: e.tensor_copy(b.ctx2[0:64, :], pM),
                   waits=[(sv, sacc_done)], inc=sv)
    cv2 = ctx.emit("v", lambda e: e.tensor_copy(
        b.ctx2[64:128, :], b.ctx2[0:64, :]), waits=[(sv, cv1)], inc=sv)
    at_done = [0] * CH
    pb_read = 0
    for c in range(CH):
        mm = 0
        for hh in range(2):
            h = 2 * c + hh
            base = 64 * (h % 2)
            for nsl in range(NSL):
                w = []
                if hh == 0 and nsl == 0:
                    w = [(sv, qt_scaled[c]), (sv, cv2), (ss, at_done[c - 1] if c else 0)]
                ctx.emit("t", lambda e, c=c, h=h, base=base, nsl=nsl, hh=hh: e.matmul(
                    b.pB[base:base + 64, ts(nsl, 512)],
                    b.ctx2[base:base + 64, ts(h, 64)],
                    QT(c)[base:base + 64, ts(nsl, 512)],
                    start=True, stop=True,
                    skip_group_check=True),
                    waits=w, inc=(st if (hh == 1 and nsl == NSL - 1) else None))
        mm = st.n
        at_done[c] = ctx.emit("s", lambda e, c=c: e.copy(QT(c), b.pB),
                              waits=[(st, mm)], inc=ss)
    if stop_after == "einsum":
        return 0
    wo = b.wsrc("L0_attn_Wo", [D, D])
    wo_ld = [0, 0]
    s_done = 0
    for dout in range(CH):
        i = dout % 2
        wr = b.wrol[:, ts(i, 1024)]
        w = [(ss, at_done[-1])] if dout == 0 else []
        if dout >= 2:
            w.append((st, wo_ld[i]))
        ld = ctx.dma("y", lambda e, dout=dout, wr=wr: e.dma_start(
            out=wr.rearrange("p (k f) -> p k f", f=128),
            in_=wo[:, ts(dout, 128)].rearrange(
                "(k p) f -> p k f", p=128)), waits=w)
        mm = 0
        for c in range(CH):
            for nsl in range(NSL):
                w2 = []
                if c == 0 and nsl == 0:
                    w2 = [ld, (sv, s_done if dout else 0)]
                    if dout == 0:
                        w2.append((ss, at_done[-1]))
                        w2.append((sv, max(tr_evac)))
                mm = ctx.emit("t", lambda e, c=c, nsl=nsl, wr=wr: e.matmul(
                    b.pA[:, ts(nsl, 512)], wr[:, ts(c, 128)],
                    QT(c)[:, ts(nsl, 512)], start=(c == 0), stop=(c == CH - 1),
                    skip_group_check=True),
                    waits=w2, inc=(st if (c == CH - 1 and nsl == NSL - 1) else None))
        wo_ld[i] = mm
        s_done = ctx.emit("v", lambda e, dout=dout: e.tensor_scalar(
            b.sb_s[:, ts(dout, N)], b.pA, bo_col[:, dout:dout + 1], None,
            ALU.add),
            waits=[(st, mm), (sv, s_consumed) if dout == 0 else (sv, 0)], inc=sv)
    b.pfree["pA"] = (ctx.sv, s_done)
    b.pfree["pB"] = (ctx.ss, at_done[-1])
    b.pfree["tr_bc"] = None
    return s_done


def emit_l0_moe_weights_pre(b):
    ctx = b.ctx
    rw = b.din("L0_ffn_router_w", [D, E0])
    return ctx.dma("g", lambda e: e.dma_start(
        out=b.rwt.rearrange("p (c e) -> p c e", e=E0),
        in_=rw.rearrange("(c p) e -> p c e", p=128)))


def emit_l0_moe(b, xn_done, g_done_req=0, n_experts=E0):
    ctx = b.ctx
    st, sv, ss, sg = ctx.st, ctx.sv, ctx.ss, ctx.sg
    w1_d = b.wsrc("L0_ffn_w1", [E0, D, F0])
    w2_d = b.wsrc("L0_ffn_w2", [E0, F0, D])
    b1c = b.bias_col("l0_b1", 2 * E0 * F0 // 256 // 1)
    b2c = b.bias_col("l0_b2", E0 * CH)
    hidT = lambda f: b.big[:, f * 1024:(f + 1) * 1024]
    w2T = lambda k: b.big[:, (16 + k) * 1024:(17 + k) * 1024]
    nmc = b.colAB[:, 0:1]
    Zc = b.colAB[:, 1:2]
    rzc = b.colAB[:, 2:3]
    tmp = [b.tokA, b.tokB]
    coefbc = [b.coefbc0, b.coefbc1]
    pAh = [b.pA[:, 0:1024], b.pA[:, 1024:2048]]
    pBh = [b.pB[:, 0:1024], b.pB[:, 1024:2048]]

    g_done = g_done_req
    w2mm_done = 0
    w1_used = [0, 0]
    hid_mm_done = 0
    bc_done = [0, 0]
    last_hev = 0
    tmp_used = [0, 0]
    pfA = b.pfree.get("pA")
    pfB = b.pfree.get("pB")
    ei = 0

    for tb in range(NTB):
        rt_mm = 0
        for s_i in range(8):
            for c in range(CH):
                w = []
                if s_i == 0 and c == 0:
                    w = [(sv, xn_done), (sv, g_done if tb else 0)]
                    if pfA and tb == 0:
                        w.append(pfA)
                    if tb == 1:
                        w.append((ss, last_hev))
                rt_mm = ctx.emit("t", lambda e, s_i=s_i, c=c, tb=tb: e.matmul(
                    b.pA[:, 14 * s_i:14 * (s_i + 1)],
                    b.xn[:, c * N + tb * TB + 128 * s_i:
                         c * N + tb * TB + 128 * (s_i + 1)],
                    b.rwt[:, 14 * c:14 * (c + 1)],
                    start=(c == 0), stop=(c == CH - 1), skip_group_check=True),
                    waits=w, inc=(st if (c == CH - 1) else None))
        vprev = 0
        sprev = 0
        for s_i in range(8):
            lgs = b.lgt[:, 14 * s_i:14 * (s_i + 1)]
            pAs = b.pA[:, 14 * s_i:14 * (s_i + 1)]
            v1 = ctx.emit("v", lambda e, pAs=pAs: e.reduce_max(
                nmc, pAs, AX.X, negate=True),
                waits=[(st, rt_mm), (sv, vprev), (ss, sprev)], inc=sv)
            s1 = ctx.emit("s", lambda e, pAs=pAs, lgs=lgs: e.activation(
                lgs, pAs, AF.Exp, bias=nmc, accum_out=Zc),
                waits=[(sv, v1)], inc=ss)
            sprev = s1
            v2 = ctx.emit("v", lambda e: e.reciprocal(rzc, Zc),
                          waits=[(ss, s1)], inc=sv)
            v3 = ctx.emit("v", lambda e, pAs=pAs, lgs=lgs: e.tensor_scalar(
                lgs, pAs, nmc, 0.0, ALU.add, ALU.is_equal),
                waits=[(sv, v2)], inc=sv)
            vprev = ctx.emit("v", lambda e, s_i=s_i, lgs=lgs: e.tensor_scalar(
                b.coefstore[:, 14 * s_i:14 * (s_i + 1)], lgs, rzc, None,
                ALU.mult), waits=[(sv, v3)], inc=sv)
        router_done = vprev

        for ex in range(n_experts):
            par = ei % 2
            w = [(sv, router_done), (sv, g_done), (ss, bc_done[par])]
            if pfB and ei == 0 and tb == 0:
                w.append(pfB)
            trc = ctx.emit("t", lambda e, ex=ex: e.transpose(
                b.pB[0:8, 0:128], b.coefstore[:, ex:112:14],
                b.ident_f32), waits=w, inc=st)
            cpc = ctx.emit("v", lambda e: e.tensor_copy(
                b.colT[0:8, :], b.pB[0:8, 0:128]), waits=[(st, trc)], inc=sv)
            bcm = 0
            for u in range(8):
                bcm = ctx.emit("t", lambda e, u=u: e.matmul(
                    b.pB[:, 1024 + 128 * u:1024 + 128 * (u + 1)],
                    b.u8[0:8, ts(u, 128)], b.colT[0:8, :],
                    start=True, stop=True, skip_group_check=True),
                    waits=[(sv, cpc)] if u == 0 else [],
                    inc=(st if u == 7 else None))
            bc_done[par] = ctx.emit("s", lambda e, par=par: e.copy(
                coefbc[par], b.pB[:, 1024:2048]), waits=[(st, bcm)], inc=ss)

            w2_tgts = []
            for k in range(16):
                w = [(st, w2mm_done)] if k == 0 else []
                w2_tgts.append(ctx.dma("y", lambda e, ex=ex, k=k: e.dma_start(
                    out=w2T(k), in_=w2_d[ex, ts(k, 128), :]), waits=w))

            hev = 0
            hev_prev2 = [0, 0]
            for f_c in range(16):
                fpar = f_c % 2
                wld = [(st, w1_used[fpar])]
                w1t = b.wrol[:, fpar * 1024:fpar * 1024 + 1024]
                ld = ctx.dma("y", lambda e, ex=ex, f_c=f_c, w1t=w1t: e.dma_start(
                    out=w1t.rearrange("p (k f) -> p k f", f=128),
                    in_=w1_d[ex, :, ts(f_c, 128)].rearrange(
                        "(k p) f -> p k f", p=128)), waits=wld)
                mm = 0
                for c in range(CH):
                    for nsl in range(2):
                        w = []
                        if c == 0 and nsl == 0:
                            w = [ld, (ss, hev_prev2[fpar]),
                                 (st, hid_mm_done if (f_c == 0 and ei > 0) else 0)]
                            if f_c == 0:
                                w.append((sv, router_done))
                        mm = ctx.emit(
                            "t", lambda e, c=c, nsl=nsl, tb=tb, fpar=fpar,
                            w1t=w1t: e.matmul(
                                pAh[fpar][:, ts(nsl, 512)],
                                w1t[:, ts(c, 128)],
                                b.xn[:, c * N + tb * TB + 512 * nsl:
                                     c * N + tb * TB + 512 * (nsl + 1)],
                                start=(c == 0), stop=(c == CH - 1),
                                skip_group_check=True),
                            waits=w,
                            inc=(st if (c == CH - 1 and nsl == 1) else None))
                w1_used[fpar] = mm
                hev = ctx.emit("s", lambda e, ex=ex, f_c=f_c, fpar=fpar: e.activation(
                    hidT(f_c), pAh[fpar], AF.Gelu,
                    bias=b1c[:, ex * 16 + f_c:ex * 16 + f_c + 1]),
                    waits=[(st, mm)], inc=ss)
                hev_prev2[fpar] = hev
                last_hev = hev
            hid_done = hev

            for dout in range(CH):
                dpar = dout % 2
                mm = 0
                for f_c in range(16):
                    for nsl in range(2):
                        w = []
                        if f_c == 0 and nsl == 0:
                            w = [(ss, hid_done), (sv, tmp_used[dpar])]
                            if dout == 0:
                                w.append((ss, bc_done[par]))
                                w += w2_tgts
                        mm = ctx.emit(
                            "t", lambda e, dout=dout, f_c=f_c, nsl=nsl,
                            dpar=dpar: e.matmul(
                                pBh[dpar][:, ts(nsl, 512)],
                                w2T(f_c)[:, ts(dout, 128)],
                                hidT(f_c)[:, ts(nsl, 512)],
                                start=(f_c == 0), stop=(f_c == 15),
                                skip_group_check=True),
                            waits=w,
                            inc=(st if (f_c == 15 and nsl == 1) else None))
                hid_mm_done = mm
                w2mm_done = mm
                gsl = b.sb_g[:, dout * N + tb * TB:dout * N + tb * TB + TB]
                if ex == 0:
                    g_done = ctx.emit(
                        "v", lambda e, dout=dout, dpar=dpar, par=par, ex=ex,
                        gsl=gsl: e.scalar_tensor_tensor(
                            gsl, pBh[dpar],
                            b2c[:, ex * CH + dout:ex * CH + dout + 1],
                            coefbc[par], ALU.add, ALU.mult),
                        waits=[(st, mm), (sv, g_done)], inc=sv)
                    tmp_used[dpar] = g_done
                else:
                    tv = ctx.emit(
                        "v", lambda e, dpar=dpar, par=par, ex=ex, dout=dout:
                        e.scalar_tensor_tensor(
                            tmp[dpar], pBh[dpar],
                            b2c[:, ex * CH + dout:ex * CH + dout + 1],
                            coefbc[par], ALU.add, ALU.mult),
                        waits=[(st, mm), (sv, tmp_used[dpar])], inc=sv)
                    g_done = ctx.emit(
                        "v", lambda e, dpar=dpar, gsl=gsl: e.tensor_tensor(
                            gsl, gsl, tmp[dpar], ALU.add),
                        waits=[(sv, tv), (sv, g_done)], inc=sv)
                    tmp_used[dpar] = g_done
            ei += 1
    b.pfree["pA"] = (sv, router_done)
    b.pfree["pB"] = (sv, g_done)
    return g_done


def emit_combine(b, xsrc_seq, xdst_seq, s_done, g_done, out_f32=True):
    ctx = b.ctx
    st, sv, ss = ctx.st, ctx.sv, ctx.ss
    xb = [b.x40, b.x41]
    used = [(sv, 0), (sv, 0)]
    stores = []
    silu_done = 0
    mul_done = 0
    add_done = 0
    for c in range(CH):
        i = c % 2
        w = [used[i]]
        if c >= 2:
            w.append(stores[c - 2])
        if c < 2:
            w += [(sv, s_done), (sv, g_done)]
        ld = ctx.dma("y", lambda e, c=c, i=i: e.dma_start(
            out=xb[i], in_=xsrc_seq[c]), waits=w)
        silu_done = ctx.emit("s", lambda e, c=c: e.activation(
            b.sg4, b.sb_g[:, ts(c, N)], AF.Silu),
            waits=[(sv, mul_done), (sv, g_done) if c == 0 else (sv, 0)], inc=ss)
        mul_done = ctx.emit("v", lambda e, c=c: e.tensor_tensor(
            b.tmp4, b.sg4, b.sb_s[:, ts(c, N)], ALU.mult),
            waits=[(ss, silu_done), (sv, add_done)], inc=sv)
        add_done = ctx.emit("v", lambda e, i=i: e.tensor_tensor(
            xb[i], xb[i], b.tmp4, ALU.add),
            waits=[(sv, mul_done), ld], inc=sv)
        pr = ctx.dma("y", lambda e, c=c, i=i: e.dma_start(
            out=xdst_seq[c], in_=xb[i]), waits=[(sv, add_done)])
        stores.append(pr)
        used[i] = (sv, add_done)
    return stores


def emit_projA(b, src_fn, dst_fn, w_dram, din_ch, dout_ch, bias, act,
               scale, t0, tlen, first_waits, src_ready, state, evac_fn=None,
               load_gate=None):
    ctx = b.ctx
    st, sv, ss = ctx.st, ctx.sv, ctx.ss
    nsl = (tlen + 511) // 512
    pfree = state.setdefault("pfree", [[], []])
    wused = state.setdefault("wused", [0, 0])
    last = 0
    last_eng = sv
    for dout in range(dout_ch):
        i = dout % 2
        wr = b.wrol[:, i * 2048:i * 2048 + din_ch * 128]
        lw = [(st, wused[i])]
        if dout == 0 and load_gate is not None:
            lw.append(load_gate)
        ld = ctx.dma("y", lambda e, dout=dout, wr=wr: e.dma_start(
            out=wr.rearrange("p (k f) -> p k f", f=128),
            in_=w_dram[:, ts(dout, 128)].rearrange("(k p) f -> p k f", p=128)),
            waits=lw)
        psum = b.pA if i == 0 else b.pB
        mm = 0
        for c in range(din_ch):
            for s_i in range(nsl):
                w = []
                if c == 0 and s_i == 0:
                    w = [ld] + list(pfree[i])
                    if src_ready is not None:
                        w.append(src_ready)
                    if dout == 0:
                        w += list(first_waits)
                mm = ctx.emit("t", lambda e, c=c, s_i=s_i, wr=wr, psum=psum:
                              e.matmul(
                    psum[:, ts(s_i, 512)], wr[:, ts(c, 128)],
                    src_fn(c)[:, s_i * 512:min(tlen, (s_i + 1) * 512)],
                    start=(c == 0), stop=(c == din_ch - 1),
                    skip_group_check=True),
                    waits=w,
                    inc=(st if (c == din_ch - 1 and s_i == nsl - 1) else None))
        wused[i] = mm
        if evac_fn is not None:
            last, last_eng = evac_fn(dout, psum, mm, i)
            pfree[i] = [(last_eng, last)]
            continue
        dst = dst_fn(dout)
        if act == "ts_add":
            last = ctx.emit("v", lambda e, dout=dout, dst=dst, psum=psum:
                            e.tensor_scalar(
                                dst, psum[:, 0:tlen],
                                bias[0][:, bias[1] + dout:bias[1] + dout + 1],
                                None if scale is None else scale,
                                ALU.add,
                                ALU.bypass if scale is None else ALU.mult),
                            waits=[(st, mm)], inc=sv)
            last_eng = sv
            pfree[i] = [(sv, last)]
        else:
            kw = {}
            if bias is not None:
                kw["bias"] = bias[0][:, bias[1] + dout:bias[1] + dout + 1]
            if scale is not None:
                kw["scale"] = scale
            last = ctx.emit("s", lambda e, dout=dout, dst=dst, psum=psum,
                            kw=kw: e.activation(
                                dst, psum[:, 0:tlen],
                                act if act is not None else AF.Copy, **kw),
                            waits=[(st, mm)], inc=ss)
            last_eng = ss
            pfree[i] = [(ss, last)]
    return last, last_eng


def emit_inner_ln(b, buf_fn, dst_fn, g_col, b_col, ready, ready_eng, state_tag):
    ctx = b.ctx
    st, sv, ss = ctx.st, ctx.sv, ctx.ss
    sum_c = b.colAB[:, 0:16]
    ssq_c = b.colAB[:, 16:32]
    mu_c = b.colAB[:, 32:48]
    rs_c = b.colAB[:, 48:64]
    pcol = [b.pB[:, 0:16], b.pB[:, 512:528]]
    pcol2 = [b.pB[:, 1024:1040], b.pB[:, 1536:1552]]
    xsq = [b.xsq0, b.xsq1]
    acc = 0
    pw = b.pfree.get("pB")
    sq_free = [0, 0]
    pcol_free = [0, 0]
    pcol2_free = [0, 0]
    for c in range(CH):
        i = c % 2
        mm = 0
        for j in range(NT):
            w = []
            if j == 0:
                w = [(sv, pcol_free[i])]
                if c == 0:
                    w.append((ready_eng, ready))
                    if pw:
                        w.append(pw)
            mm = ctx.emit("t", lambda e, c=c, j=j, i=i: e.matmul(
                pcol[i][:, j:j + 1], buf_fn(c)[:, ts(j, 128)], b.ones_col,
                start=True, stop=True, skip_group_check=True),
                waits=w, inc=(st if j == NT - 1 else None))
        sq = ctx.emit("s", lambda e, c=c, i=i: e.activation(
            xsq[i], buf_fn(c), AF.Square),
            waits=[(st, sq_free[i]), (ready_eng, ready) if c == 0 else (sv, 0)],
            inc=ss)
        mm2 = 0
        for j in range(NT):
            w = [(ss, sq), (sv, pcol2_free[i])] if j == 0 else []
            mm2 = ctx.emit("t", lambda e, j=j, i=i: e.matmul(
                pcol2[i][:, j:j + 1], xsq[i][:, ts(j, 128)], b.ones_col,
                start=True, stop=True, skip_group_check=True),
                waits=w, inc=(st if j == NT - 1 else None))
        sq_free[i] = mm2
        if c == 0:
            a1 = ctx.emit("v", lambda e, i=i: e.tensor_copy(sum_c, pcol[i]),
                          waits=[(st, mm)], inc=sv)
            acc = ctx.emit("v", lambda e, i=i: e.tensor_copy(ssq_c, pcol2[i]),
                           waits=[(st, mm2), (sv, a1)], inc=sv)
        else:
            a1 = ctx.emit("v", lambda e, i=i: e.tensor_tensor(
                sum_c, sum_c, pcol[i], ALU.add), waits=[(st, mm), (sv, acc)],
                inc=sv)
            acc = ctx.emit("v", lambda e, i=i: e.tensor_tensor(
                ssq_c, ssq_c, pcol2[i], ALU.add),
                waits=[(st, mm2), (sv, a1)], inc=sv)
        pcol_free[i] = a1
        pcol2_free[i] = acc
    v1 = ctx.emit("v", lambda e: e.tensor_scalar(
        mu_c, sum_c, 1.0 / D, None, ALU.mult), waits=[(sv, acc)], inc=sv)
    s1 = ctx.emit("s", lambda e: e.activation(sum_c, mu_c, AF.Square),
                  waits=[(sv, v1)], inc=ss)
    v2 = ctx.emit("v", lambda e: e.scalar_tensor_tensor(
        rs_c, ssq_c, 1.0 / D, sum_c, ALU.mult, ALU.subtract),
        waits=[(ss, s1)], inc=sv)
    v3 = ctx.emit("v", lambda e: e.tensor_scalar(
        rs_c, rs_c, EPS, None, ALU.add), waits=[(sv, v2)], inc=sv)
    s2 = ctx.emit("s", lambda e: e.activation(rs_c, rs_c, AF.Ln),
                  waits=[(sv, v3)], inc=ss)
    s3 = ctx.emit("s", lambda e: e.activation(rs_c, rs_c, AF.Exp, scale=-0.5),
                  waits=[(ss, s2)], inc=ss)
    bc1 = emit_col_broadcast(b, mu_c, b.bcmu, [(sv, v1)])
    bc2 = emit_col_broadcast(b, rs_c, b.bcrs, [(ss, s3)])
    done = 0
    for c in range(CH):
        w = [(ss, bc2), (ss, bc1)] if c == 0 else []
        d1 = ctx.emit("v", lambda e, c=c: e.tensor_tensor(
            b.scr32, buf_fn(c), b.bcmu, ALU.subtract), waits=w + [(sv, done)],
            inc=sv)
        d2 = ctx.emit("v", lambda e, c=c: e.tensor_tensor(
            b.scr32, b.scr32, b.bcrs, ALU.mult), waits=[(sv, d1)], inc=sv)
        done = ctx.emit("v", lambda e, c=c: e.tensor_scalar(
            dst_fn(c), b.scr32, g_col[:, c:c + 1], b_col[:, c:c + 1],
            ALU.mult, ALU.add), waits=[(sv, d2)], inc=sv)
    b.pfree["pB"] = (ctx.sv, pcol2_free[1] if pcol2_free[1] else acc)
    return done


def emit_l1_attn(b, xn_done, s_consumed=0, g_consumed=0, nh_lim=16):
    ctx = b.ctx
    st, sv, ss = ctx.st, ctx.sv, ctx.ss
    wq = b.wsrc("L1_attn_Wq", [D, D])
    wkv = b.wsrc("L1_attn_Wkv", [D, 128])
    wo = b.wsrc("L1_attn_Wo", [D, D])
    g1c = b.bias_col("ln1_g", 8)
    b1c = b.bias_col("ln1_b", 8)
    g2c = b.bias_col("ln2_g", 8)
    b2c = b.bias_col("ln2_b", 8)
    xln = lambda c: b.big[:, ts(c, 2048)]
    qfm = lambda c: b.big[:, (CH + c) * 2048:(CH + c + 1) * 2048]
    ln1 = emit_inner_ln(b, lambda c: b.xn[:, ts(c, N)], xln, g1c, b1c,
                        xn_done, sv, "ln1")
    stq = {}
    pfA = b.pfree.get("pA")
    fw = [pfA] if pfA else []
    qdone, qeng = emit_projA(b, xln, qfm, wq, CH, CH, None, None, SCALE_L1,
                             0, N, fw, (sv, ln1), stq, load_gate=(sv, ln1))
    ldkv = ctx.dma("y", lambda e: e.dma_start(
        out=b.wrol[:, 2048:3072].rearrange("p (k f) -> p k f", f=128),
        in_=wkv.rearrange("(k p) f -> p k f", p=128)),
        waits=[(st, stq["wused"][1])])
    kvmm = 0
    for c in range(CH):
        for s_i in range(NSL):
            w = []
            if c == 0 and s_i == 0:
                w = [ldkv, (qeng, qdone), (st, stq["wused"][0])]
            kvmm = ctx.emit("t", lambda e, c=c, s_i=s_i: e.matmul(
                b.pA[:, ts(s_i, 512)],
                b.wrol[:, 2048 + 128 * c:2048 + 128 * (c + 1)],
                xln(c)[:, ts(s_i, 512)], start=(c == 0), stop=(c == CH - 1),
                skip_group_check=True), waits=w,
                inc=(st if (c == CH - 1 and s_i == NSL - 1) else None))
    k1 = ctx.emit("v", lambda e: e.tensor_copy(b.kk[0:64, :], b.pA[0:64, 0:2048]),
                  waits=[(st, kvmm)], inc=sv)
    k2 = ctx.emit("v", lambda e: e.tensor_copy(b.kk[64:128, :], b.kk[0:64, :]),
                  waits=[(sv, k1)], inc=sv)
    vf = ctx.emit("s", lambda e: e.copy(b.vf32[64:128, :], b.pA[64:128, 0:2048]),
                  waits=[(st, kvmm)], inc=ss)
    trv = 0
    for j in range(NT):
        w = [(ss, vf), (sv, k1)] if j == 0 else []
        if j == 0 and b.pfree.get("pB"):
            w.append(b.pfree["pB"])
        trv = ctx.emit("t", lambda e, j=j: e.transpose(
            b.pB[:, 64 * j:64 * (j + 1)], b.vf32[64:128, ts(j, 128)],
            b.ident_f32[64:128, 64:128]), waits=w, inc=st)
    vcp = ctx.emit("v", lambda e: e.tensor_copy(
        b.vaug.rearrange("p (h e) -> p h e", e=72)[:, :, 0:64],
        b.pB[:, 0:1024].rearrange("p (j x) -> p j x", x=64)),
        waits=[(st, trv)], inc=sv)
    vone = ctx.emit("v", lambda e: e.memset(
        b.vaug.rearrange("p (h e) -> p h e", e=72)[:, :, 64:65], 1.0),
        waits=[(sv, vcp)], inc=sv)
    pt = [b.pt0, b.pt1]
    pAh = [b.pA[:, 0:1024], b.pA[:, 1024:2048]]
    pBh = [b.pB[:, 0:1024], b.pB[:, 1024:2048]]
    exp_done = [0, 0]
    pv_done = [0, 0]
    norm_done = [0, 0]
    rec_row, _ = b.bias_row("l1_rec", 1024)
    oi = 0
    s1_ready = 0
    for h in range(nh_lim):
        hb = 64 * (h % 2)
        for ihalf in range(2):
            op = oi % 2
            io0 = ihalf * 1024
            lastmm = 0
            for j in range(NT):
                jp = j % 2
                w = []
                if j <= 1:
                    w = [(sv, vone), (qeng, qdone), (sv, k2),
                         (sv, norm_done[op]), (ss, exp_done[jp])]
                    if h == 0 and ihalf == 0 and j == 0:
                        w.append((sv, g_consumed))
                else:
                    w.append((ss, exp_done[jp]))
                stmm = 0
                for s_i in range(2):
                    ww = w if s_i == 0 else []
                    stmm = ctx.emit("t", lambda e, h=h, hb=hb, j=j, jp=jp,
                                    s_i=s_i, io0=io0: e.matmul(
                                        pAh[jp][:, ts(s_i, 512)],
                                        b.kk[hb:hb + 64, ts(j, 128)],
                                        qfm(h // 2)[hb:hb + 64, io0 + 512 * s_i:
                                                    io0 + 512 * (s_i + 1)],
                                        start=True, stop=True,
                                        skip_group_check=True),
                                    waits=ww, inc=(st if s_i == 1 else None))
                ed = ctx.emit("s", lambda e, jp=jp: e.activation(
                    pt[jp][:, 0:1024], pAh[jp], AF.Exp),
                    waits=[(st, stmm), (st, pv_done[jp]),
                           (sv, norm_done[0]), (sv, norm_done[1])], inc=ss)
                exp_done[jp] = ed
                for s_i in range(2):
                    w2 = [(ss, ed)] if s_i == 0 else []
                    lastmm = ctx.emit("t", lambda e, j=j, jp=jp, s_i=s_i,
                                      op=op: e.matmul(
                                          pBh[op][0:65, ts(s_i, 512)],
                                          b.vaug[:, 72 * j:72 * j + 65],
                                          pt[jp][:, ts(s_i, 512)],
                                          start=(j == 0), stop=(j == NT - 1),
                                          skip_group_check=True),
                                      waits=w2,
                                      inc=(st if s_i == 1 else None))
                pv_done[jp] = lastmm
            l1 = ctx.emit("s", lambda e, op=op: e.activation(
                b.rows_a[64:65, 0:1024], pBh[op][64:65, :], AF.Ln),
                waits=[(st, lastmm)], inc=ss)
            l2 = ctx.emit("s", lambda e: e.activation(
                b.rows_a[64:65, 0:1024], b.rows_a[64:65, 0:1024], AF.Exp,
                scale=-1.0), waits=[(ss, l1)], inc=ss)
            l3 = ctx.emit("s", lambda e: e.copy(rec_row, b.rows_a[64:65, 0:1024]),
                          waits=[(ss, l2)], inc=ss)
            bcm = 0
            for s_i in range(2):
                w2 = [(ss, l3), (ss, exp_done[0]), (ss, exp_done[1])] \
                    if s_i == 0 else []
                bcm = ctx.emit("t", lambda e, s_i=s_i: e.matmul(
                    b.pA[0:64, ts(s_i, 512)], b.ones_row[32:33, 0:64],
                    rec_row[0:1, ts(s_i, 512)], start=True, stop=True,
                    skip_group_check=True), waits=w2,
                    inc=(st if s_i == 1 else None))
            rb = ctx.emit("s", lambda e: e.copy(b.recbc[0:64, 0:1024],
                                                b.pA[0:64, 0:1024]),
                          waits=[(st, bcm)], inc=ss)
            norm_done[op] = ctx.emit(
                "v", lambda e, h=h, hb=hb, io0=io0, op=op: e.tensor_tensor(
                    b.sb_g[hb:hb + 64, (h // 2) * 2048 + io0:
                           (h // 2) * 2048 + io0 + 1024],
                    pBh[op][0:64, :], b.recbc[0:64, 0:1024], ALU.mult),
                waits=[(ss, rb)], inc=sv)
            exp_done[0] = max(exp_done[0], 0)
            s1_ready = norm_done[op]
            oi += 1
            exp_done[0] = rb if rb > exp_done[0] else exp_done[0]
            exp_done[1] = rb if rb > exp_done[1] else exp_done[1]
    attnT_done = s1_ready
    stw = {"wused": stq["wused"]}
    wodone, woeng = emit_projA(
        b, lambda c: b.sb_g[:, ts(c, 2048)], lambda c: b.sb_s[:, ts(c, N)],
        wo, CH, CH, None, None, None, 0, N,
        [(sv, attnT_done), (sv, s_consumed)], (sv, attnT_done), stw,
        load_gate=(sv, vone))
    ln2 = emit_inner_ln(b, lambda c: b.sb_s[:, ts(c, N)],
                        lambda c: b.sb_s[:, ts(c, N)], g2c, b2c,
                        wodone, woeng, "ln2")
    b.pfree["pA"] = (sv, ln2)
    b.pfree["pB"] = (sv, ln2)
    b.l1_wused = stw["wused"]
    return ln2


def emit_coef_bc8(b, src_strided, dst_bf, waits):
    ctx = b.ctx
    st, sv, ss = ctx.st, ctx.sv, ctx.ss
    trc = ctx.emit("t", lambda e: e.transpose(
        b.pB[0:8, 0:128], src_strided, b.ident_f32), waits=waits, inc=st)
    cpc = ctx.emit("v", lambda e: e.tensor_copy(
        b.colT[0:8, :], b.pB[0:8, 0:128]), waits=[(st, trc)], inc=sv)
    bcm = 0
    for u in range(8):
        bcm = ctx.emit("t", lambda e, u=u: e.matmul(
            b.pB[:, 1024 + 128 * u:1024 + 128 * (u + 1)],
            b.u8[0:8, ts(u, 128)], b.colT[0:8, :], start=True, stop=True,
            skip_group_check=True),
            waits=[(sv, cpc)] if u == 0 else [], inc=(st if u == 7 else None))
    ev = ctx.emit("s", lambda e, dst_bf=dst_bf: e.copy(
        dst_bf, b.pB[:, 1024:2048]), waits=[(st, bcm)], inc=ss)
    return ev


def emit_l1_ffn(b, xn_done, g_gate=0):
    ctx = b.ctx
    st, sv, ss, sg = ctx.st, ctx.sv, ctx.ss, ctx.sg
    bigA = lambda c: b.big[:, ts(c, 2048)]
    bigB = lambda c: b.big[:, (CH + c) * 2048:(CH + c + 1) * 2048]
    bigBh = lambda f: b.big[:, 16384 + f * 1024:16384 + (f + 1) * 1024]
    bigAh = lambda f: b.big[:, f * 1024:(f + 1) * 1024]
    rw1 = b.din("L1_ffn_router_w", [D, 4])
    rb_d = b.din("L1_ffn_router_b", [4])
    lb_d = b.din("L1_ffn_load_balancer", [4])
    sw1 = b.wsrc("L1_ffn_sw_w1", [D, I1])
    sw3 = b.wsrc("L1_ffn_sw_w3", [D, I1])
    sw2 = b.wsrc("L1_ffn_sw_w2", [I1, D])
    eqw = b.wsrc("L1_ffn_me_eq_w", [D, D])
    mewv = b.wsrc("L1_ffn_me_wv", [D, D])
    mewo = b.wsrc("L1_ffn_me_wo", [D, D])
    c1w = b.wsrc("L1_ffn_me_c1w", [D, 2 * D])
    c2w = b.wsrc("L1_ffn_me_c2w", [2 * D, D])
    synw = b.wsrc("L1_ffn_ce_syn_w", [D, D])
    cewv = b.wsrc("L1_ffn_ce_wv", [D, D])
    cewo = b.wsrc("L1_ffn_ce_wo", [D, D])
    f1w = b.wsrc("L1_ffn_ce_f1w", [D, 2 * D])
    f2w = b.wsrc("L1_ffn_ce_f2w", [2 * D, D])
    genw = b.wsrc("L1_ffn_ce_gen_w", [D, D])
    mlw1 = b.wsrc("L1_ffn_ml_w1", [D, I1])
    mlw2 = b.wsrc("L1_ffn_ml_w2", [I1, D])
    eq_b = b.bias_col("me_eq_b", 8)
    me_bv = b.bias_col("me_bv", 8)
    me_bo = b.bias_col("me_bo", 8)
    c1b = b.bias_col("me_c1b", 16)
    c2b = b.bias_col("me_c2b", 8)
    syn_b = b.bias_col("ce_syn_b", 8)
    ce_bv = b.bias_col("ce_bv", 8)
    ce_bo = b.bias_col("ce_bo", 8)
    n1g = b.bias_col("ce_n1g", 8)
    n1b = b.bias_col("ce_n1b", 8)
    f1b = b.bias_col("ce_f1b", 16)
    f2b = b.bias_col("ce_f2b", 8)
    n2g = b.bias_col("ce_n2g", 8)
    n2b = b.bias_col("ce_n2b", 8)
    gen_b = b.bias_col("ce_gen_b", 8)
    ml_b1 = b.bias_col("ml_b1", 32)
    ml_b2 = b.bias_col("ml_b2", 8)
    rb_row, _ = b.bias_row("l1_rb4", 4, off=2000)

    d1 = ctx.dma("y", lambda e: e.dma_start(
        out=b.rows_a[0:1, 0:4], in_=rb_d.rearrange("(o f) -> o f", o=1)))
    d2 = ctx.dma("y", lambda e: e.dma_start(
        out=b.rows_a[0:1, 4:8], in_=lb_d.rearrange("(o f) -> o f", o=1)))
    v1 = ctx.emit("v", lambda e: e.tensor_tensor(
        b.rows_a[0:1, 8:12], b.rows_a[0:1, 0:4], b.rows_a[0:1, 4:8], ALU.add),
        waits=[d1, d2], inc=sv)
    s1 = ctx.emit("s", lambda e: e.copy(rb_row, b.rows_a[0:1, 8:12]),
                  waits=[(sv, v1)], inc=ss)

    pfA = b.pfree.get("pA")
    rt_mm = 0
    for s_i in range(NT):
        for c in range(CH):
            w = []
            if s_i == 0 and c == 0:
                w = [(sv, xn_done), (ss, s1)]
                if pfA:
                    w.append(pfA)
            ctx.emit("t", lambda e, s_i=s_i, c=c: e.matmul(
                b.pA[:, 4 * s_i:4 * (s_i + 1)],
                b.xn[:, c * N + 128 * s_i:c * N + 128 * (s_i + 1)],
                b.rwt1[:, 4 * c:4 * (c + 1)],
                start=(c == 0), stop=False, skip_group_check=True), waits=w)
        rt_mm = ctx.emit("t", lambda e, s_i=s_i: e.matmul(
            b.pA[:, 4 * s_i:4 * (s_i + 1)], b.ones_row[64:65, :],
            rb_row, start=False, stop=True, skip_group_check=True),
            waits=[], inc=st)
    vprev = ctx.emit("v", lambda e: e.memset(b.p7_coefstore, 0.0),
                     waits=[(sv, g_gate)], inc=sv)
    sprev = 0
    for s_i in range(NT):
        L = b.p7_lgt[:, 16 * s_i:16 * (s_i + 1)]
        q = b.pA[:, 4 * s_i:4 * (s_i + 1)]

        def col(k, L=L):
            return L[:, k:k + 1]
        v0 = ctx.emit("v", lambda e, L=L, q=q: e.tensor_copy(L[:, 0:4], q),
                      waits=[(st, rt_mm), (sv, vprev), (ss, sprev)], inc=sv)
        ops = [
            lambda e, L=L: e.tensor_tensor(col(4, L), col(0, L), col(1, L), ALU.max),
            lambda e, L=L: e.tensor_tensor(col(5, L), col(0, L), col(1, L), ALU.min),
            lambda e, L=L: e.tensor_tensor(col(6, L), col(2, L), col(3, L), ALU.max),
            lambda e, L=L: e.tensor_tensor(col(7, L), col(2, L), col(3, L), ALU.min),
            lambda e, L=L: e.tensor_tensor(col(8, L), col(4, L), col(6, L), ALU.max),
            lambda e, L=L: e.tensor_tensor(col(9, L), col(4, L), col(6, L), ALU.min),
            lambda e, L=L: e.tensor_tensor(col(10, L), col(5, L), col(7, L), ALU.max),
            lambda e, L=L: e.tensor_tensor(col(12, L), col(9, L), col(10, L), ALU.max),
        ]
        vv = v0
        for f in ops:
            vv = ctx.emit("v", f, waits=[(sv, vv)], inc=sv)
        vv = ctx.emit("v", lambda e, L=L: e.tensor_tensor(
            col(13, L), col(12, L), col(8, L), ALU.subtract),
            waits=[(sv, vv)], inc=sv)
        sx = ctx.emit("s", lambda e, L=L: e.activation(
            col(13, L), col(13, L), AF.Exp), waits=[(sv, vv)], inc=ss)
        vv = ctx.emit("v", lambda e, L=L: e.tensor_scalar(
            col(13, L), col(13, L), 1.0, None, ALU.add),
            waits=[(ss, sx)], inc=sv)
        vv = ctx.emit("v", lambda e, L=L: e.reciprocal(col(14, L), col(13, L)),
                      waits=[(sv, vv)], inc=sv)
        for ex in range(4):
            vv = ctx.emit("v", lambda e, L=L, ex=ex: e.tensor_tensor(
                col(15, L), col(ex, L), col(8, L), ALU.subtract),
                waits=[(sv, vv)], inc=sv)
            sx = ctx.emit("s", lambda e, L=L: e.activation(
                col(15, L), col(15, L), AF.Exp), waits=[(sv, vv)], inc=ss)
            vv = ctx.emit("v", lambda e, L=L, ex=ex, s_i=s_i: e.tensor_tensor(
                b.p7_coefstore[:, 4 * s_i + ex:4 * s_i + ex + 1],
                col(ex, L), col(12, L), ALU.is_ge), waits=[(ss, sx)], inc=sv)
            vv = ctx.emit("v", lambda e, L=L, ex=ex, s_i=s_i:
                          e.scalar_tensor_tensor(
                              b.p7_coefstore[:, 4 * s_i + ex:4 * s_i + ex + 1],
                              col(15, L), col(14, L),
                              b.p7_coefstore[:, 4 * s_i + ex:4 * s_i + ex + 1],
                              ALU.mult, ALU.mult), waits=[(sv, vv)], inc=sv)
            sprev = sx
        vprev = vv
    router_done = vprev

    g_done = g_gate
    bc_done = [0, 0]
    tmp_used = [0, 0]
    ei = 0

    def g_update(dout, psum, mm, wpar, ex, tb, bias_col, bias_base, first):
        nonlocal g_done, ei
        ct = coef_tile[0]
        gsl = b.sb_g[:, dout * N + tb * TB:dout * N + tb * TB + TB]
        if first:
            tgt = ctx.emit("v", lambda e, dout=dout, psum=psum, ct=ct: (
                e.scalar_tensor_tensor(
                    gsl, psum[:, 0:TB],
                    bias_col[:, bias_base + dout:bias_base + dout + 1],
                    ct, ALU.add, ALU.mult)
                if bias_col is not None else
                e.tensor_tensor(gsl, psum[:, 0:TB], ct, ALU.mult)),
                waits=[(st, mm), (sv, g_done)], inc=sv)
            g_done = tgt
            return tgt, sv
        tpar = dout % 2
        tv = ctx.emit("v", lambda e, dout=dout, psum=psum, tpar=tpar, ct=ct: (
            e.scalar_tensor_tensor(
                b.p7_tmp[tpar], psum[:, 0:TB],
                bias_col[:, bias_base + dout:bias_base + dout + 1],
                ct, ALU.add, ALU.mult)
            if bias_col is not None else
            e.tensor_tensor(b.p7_tmp[tpar], psum[:, 0:TB], ct, ALU.mult)),
            waits=[(st, mm), (sv, tmp_used[tpar])], inc=sv)
        tgt = ctx.emit("v", lambda e, tpar=tpar, gsl=gsl: e.tensor_tensor(
            gsl, gsl, b.p7_tmp[tpar], ALU.add),
            waits=[(sv, tv), (sv, g_done)], inc=sv)
        tmp_used[tpar] = tgt
        g_done = tgt
        return tgt, sv

    coef_tile = [None]

    stA = {}
    psums4 = [b.pA[:, 0:1024], b.pA[:, 1024:2048],
              b.pB[:, 0:1024], b.pB[:, 1024:2048]]
    amm = 0
    sil_done = 0
    mul_done = 0
    hev2 = [0, 0]
    wused4 = [0, 0, 0, 0]
    p2_evac = 0
    for ex, (wa, wb, w2d, hb1, hb1b, hb2, hb2b, swiglu) in enumerate([
            (sw1, sw3, sw2, None, 0, None, 0, True),
            (mlw1, None, mlw2, ml_b1, 0, ml_b2, 0, False)]):
        exn = 0 if swiglu else 3
        for tb in range(NTB):
            par = ei % 2
            bc_w = [(sv, router_done), (ss, bc_done[par]), (sv, g_done),
                    (st, p2_evac if False else 0)]
            bc_done[par] = emit_coef_bc8(
                b, b.p7_coefstore[:, 32 * tb + exn:32 * tb + 32:4],
                b.p7_bc[par], bc_w)
            coef_tile[0] = b.p7_bc[par]
            for f_c in range(32):
                i = f_c % 2
                wsl = b.wrol[:, i * 1024:(i + 1) * 1024]
                wsl3 = b.wrol[:, 2048 + i * 1024:2048 + (i + 1) * 1024]
                lw = [(st, wused4[i])]
                ld1 = ctx.dma("y", lambda e, f_c=f_c, wsl=wsl, wa=wa: e.dma_start(
                    out=wsl.rearrange("p (k f) -> p k f", f=128),
                    in_=wa[:, ts(f_c, 128)].rearrange("(k p) f -> p k f", p=128)),
                    waits=lw)
                if swiglu:
                    ld3 = ctx.dma("y", lambda e, f_c=f_c, wsl3=wsl3, wb=wb:
                                  e.dma_start(
                                      out=wsl3.rearrange("p (k f) -> p k f", f=128),
                                      in_=wb[:, ts(f_c, 128)].rearrange(
                                          "(k p) f -> p k f", p=128)),
                                  waits=[(st, wused4[2 + i])])
                mm = mm3 = 0
                for c in range(CH):
                    for s_i in range(2):
                        w = []
                        if c == 0 and s_i == 0:
                            w = [ld1, (ss, sil_done if swiglu else hev2[0]),
                                 (ss, bc_done[par])]
                            if not swiglu:
                                w = [ld1, (ss, hev2[i]), (ss, bc_done[par])]
                        mm = ctx.emit("t", lambda e, c=c, s_i=s_i, tb=tb,
                                      wsl=wsl, i=i, swiglu=swiglu: e.matmul(
                                          (psums4[0] if swiglu else psums4[i])
                                          [:, ts(s_i, 512)],
                                          wsl[:, ts(c, 128)],
                                          b.xn[:, c * N + tb * TB + 512 * s_i:
                                               c * N + tb * TB + 512 * (s_i + 1)],
                                          start=(c == 0), stop=(c == CH - 1),
                                          skip_group_check=True),
                                      waits=w,
                                      inc=(st if c == CH - 1 and s_i == 1 else None))
                wused4[i] = mm
                if swiglu:
                    for c in range(CH):
                        for s_i in range(2):
                            w = [ld3, (sv, mul_done)] if (c == 0 and s_i == 0) else []
                            mm3 = ctx.emit("t", lambda e, c=c, s_i=s_i, tb=tb,
                                           wsl3=wsl3: e.matmul(
                                               psums4[1][:, ts(s_i, 512)],
                                               wsl3[:, ts(c, 128)],
                                               b.xn[:, c * N + tb * TB + 512 * s_i:
                                                    c * N + tb * TB + 512 * (s_i + 1)],
                                               start=(c == 0), stop=(c == CH - 1),
                                               skip_group_check=True),
                                           waits=w,
                                           inc=(st if c == CH - 1 and s_i == 1 else None))
                    wused4[2 + i] = mm3
                    sil_done = ctx.emit("s", lambda e: e.activation(
                        b.p7_tmp[0], psums4[0], AF.Silu),
                        waits=[(st, mm), (sv, mul_done)], inc=ss)
                    mul_done = ctx.emit("v", lambda e, f_c=f_c: e.tensor_tensor(
                        bigAh(f_c), b.p7_tmp[0], psums4[1],
                        ALU.mult), waits=[(ss, sil_done), (st, mm3)], inc=sv)
                else:
                    hev2[i] = ctx.emit("s", lambda e, f_c=f_c, i=i: e.activation(
                        bigAh(f_c), psums4[i], AF.Gelu,
                        bias=hb1[:, f_c:f_c + 1]),
                        waits=[(st, mm)], inc=ss)
            hid_ready = (sv, mul_done) if swiglu else (ss, hev2[(32 - 1) % 2])
            if swiglu:
                tmp_used[0] = max(tmp_used[0], mul_done)
            for grp in range(2):
                for f_c in range(32):
                    i = f_c % 2
                    wsl = b.wrol[:, i * 1024:(i + 1) * 1024]
                    lw = [(st, wused4[i])]
                    if grp == 0 and f_c < 2:
                        lw.append(hid_ready)
                    ld = ctx.dma("y", lambda e, f_c=f_c, wsl=wsl, w2d=w2d:
                                 e.dma_start(
                                     out=wsl, in_=w2d[ts(f_c, 128), :]),
                                 waits=lw)
                    mm = 0
                    for dg in range(4):
                        for s_i in range(2):
                            w = []
                            if dg == 0 and s_i == 0:
                                w = [ld, hid_ready]
                                if f_c == 0:
                                    w.append((sv, g_done))
                                    w.append((sv, tmp_used[0]))
                                    w.append((sv, tmp_used[1]))
                                    w.append((ss, bc_done[par]))
                            mm = ctx.emit("t", lambda e, grp=grp, dg=dg,
                                          s_i=s_i, f_c=f_c, wsl=wsl: e.matmul(
                                              psums4[dg][:, ts(s_i, 512)],
                                              wsl[:, 128 * (grp * 4 + dg):
                                                  128 * (grp * 4 + dg + 1)],
                                              bigAh(f_c)[:, ts(s_i, 512)],
                                              start=(f_c == 0), stop=(f_c == 31),
                                              skip_group_check=True),
                                          waits=w,
                                          inc=(st if (dg == 3 and s_i == 1) else None))
                    wused4[i] = mm
                for dg in range(4):
                    t1, _e = g_update(grp * 4 + dg, psums4[dg], mm, None,
                                      exn, tb, hb2, 0, ei < 2)
                    p2_evac = t1
            ei += 1
    e03_done = g_done

    stP = {"wused": [wused4[0], wused4[1]],
           "pfree": [[(sv, g_done)], [(sv, g_done)]]}

    def chain_proj(src_fn, dst_fn, w, bias, act, gate, ready, ready_eng,
                   tlen=N, din=CH, dout=CH, evac_fn=None):
        stP_local = stP
        done, eng = emit_projA(b, src_fn, dst_fn, w, din, dout, bias, act,
                               None, 0, tlen, [(ready_eng, ready)],
                               (ready_eng, ready),
                               stP_local, evac_fn=evac_fn, load_gate=gate)
        return done, eng

    def coef_bc(exn, tb, extra_waits):
        par = ei % 2
        w = [(sv, router_done), (ss, bc_done[par])] + extra_waits
        if "pfree" in stP:
            w += list(stP["pfree"][0]) + list(stP["pfree"][1])
        tgt = emit_coef_bc8(
            b, b.p7_coefstore[:, 32 * tb + exn:32 * tb + 32:4],
            b.p7_bc[par], w)
        bc_done[par] = tgt
        if "pfree" in stP:
            stP["pfree"][0] = list(stP["pfree"][0]) + [(ss, tgt)]
            stP["pfree"][1] = list(stP["pfree"][1]) + [(ss, tgt)]
        return par, tgt

    r = (sv, e03_done)
    eqd, eqe = chain_proj(lambda c: b.xn[:, ts(c, N)], bigB, eqw,
                          (eq_b, 0), "ts_add", r, e03_done, sv)
    t1d, t1e = chain_proj(bigB, bigA, mewv, (me_bv, 0), "ts_add",
                          (eqe, eqd), eqd, eqe)
    syd, sye = chain_proj(bigA, bigB, mewo, (me_bo, 0), "ts_add",
                          (t1e, t1d), t1d, t1e)
    for tb in range(NTB):
        par, bct = coef_bc(1, tb, [(sv, g_done)])
        coef_tile[0] = b.p7_bc[par]
        h1d, h1e = chain_proj(
            lambda c, tb=tb: bigB(c)[:, tb * TB:(tb + 1) * TB],
            lambda f: bigAh(f), c1w, (c1b, 0), AF.Gelu,
            (sye, syd), syd, sye, tlen=TB, din=CH, dout=16)

        def evac_e1(dout, psum, mm, wpar, tb=tb, par=par):
            return g_update(dout, psum, mm, wpar, 1, tb, c2b, 0, False)
        g1d, g1e = chain_proj(
            lambda k: bigAh(k), None, c2w, None, None,
            (h1e, h1d), h1d, h1e, tlen=TB, din=16, dout=CH, evac_fn=evac_e1)
        ei += 1

    snd, sne = chain_proj(lambda c: b.xn[:, ts(c, N)], bigB, synw,
                          (syn_b, 0), "ts_add", (sv, g_done), g_done, sv)
    u1d, u1e = chain_proj(bigB, bigA, cewv, (ce_bv, 0), "ts_add",
                          (sne, snd), snd, sne)

    def evac_add_syn(dout, psum, mm, wpar):
        tgt = ctx.emit("v", lambda e, dout=dout, psum=psum:
                       e.scalar_tensor_tensor(
                           bigB(dout), psum, ce_bo[:, dout:dout + 1],
                           bigB(dout), ALU.add, ALU.add),
                       waits=[(st, mm)], inc=sv)
        return tgt, sv
    h2d, h2e = chain_proj(bigA, None, cewo, None, None, (u1e, u1d), u1d, u1e,
                          evac_fn=evac_add_syn)
    ln1d = emit_inner_ln(b, bigB, bigB, n1g, n1b, h2d, h2e, "e2ln1")
    for tb in range(NTB):
        f1d, f1e = chain_proj(
            lambda c, tb=tb: bigB(c)[:, tb * TB:(tb + 1) * TB],
            lambda f: bigAh(f), f1w, (f1b, 0), AF.Relu,
            (sv, ln1d), ln1d, sv, tlen=TB, din=CH, dout=16)

        def evac_add_h2(dout, psum, mm, wpar, tb=tb):
            tgt = ctx.emit("v", lambda e, dout=dout, psum=psum, tb=tb:
                           e.scalar_tensor_tensor(
                               bigB(dout)[:, tb * TB:(tb + 1) * TB],
                               psum[:, 0:TB], f2b[:, dout:dout + 1],
                               bigB(dout)[:, tb * TB:(tb + 1) * TB],
                               ALU.add, ALU.add),
                           waits=[(st, mm)], inc=sv)
            return tgt, sv
        f2d, f2e = chain_proj(
            lambda k: bigAh(k), None, f2w, None, None,
            (f1e, f1d), f1d, f1e, tlen=TB, din=16, dout=CH,
            evac_fn=evac_add_h2)
        ln1d = f2d
    ln2d = emit_inner_ln(b, bigB, bigB, n2g, n2b, ln1d, sv, "e2ln2")
    for tb in range(NTB):
        par, bct = coef_bc(2, tb, [(sv, g_done), (sv, ln2d)])
        coef_tile[0] = b.p7_bc[par]

        def evac_e2(dout, psum, mm, wpar, tb=tb):
            return g_update(dout, psum, mm, wpar, 2, tb, gen_b, 0, False)
        g2d, g2e = chain_proj(
            lambda c, tb=tb: bigB(c)[:, tb * TB:(tb + 1) * TB],
            None, genw, None, None, (sv, ln2d), ln2d, sv,
            tlen=TB, din=CH, dout=CH, evac_fn=evac_e2)
        ei += 1

    b.pfree["pA"] = (sv, g_done)
    b.pfree["pB"] = (sv, g_done)
    return g_done


def emit_store_fm(b, sbuf_ap, dram_seq, done_sem, done_val):
    ctx = b.ctx
    prs = []
    for c in range(CH):
        prs.append(ctx.dma("y", lambda e, c=c: e.dma_start(
            out=dram_seq[c], in_=sbuf_ap[:, ts(c, N)]),
            waits=[(done_sem, done_val)] if c == 0 else []))
    return prs


def emit_load_fm(b, sbuf_ap, dram_seq, waits=()):
    ctx = b.ctx
    prs = []
    for c in range(CH):
        prs.append(ctx.dma("g", lambda e, c=c: e.dma_start(
            out=sbuf_ap[:, ts(c, N)], in_=dram_seq[c]),
            waits=waits if c == 0 else []))
    return prs


def emit_final_drain(b):
    ctx = b.ctx
    waits = tuple((s.h, s.n) for s in ctx.all_sems() if s.n > 0)
    ctx.q["g"].append((None, waits, None, 0))
    ctx.q["g"].append((lambda e: e.nop(), (), None, 0))


def finalize_and_run(b, in_maps, core_ids=None, **kw):
    with b.nc.Block() as block:
        b.ctx.finalize(block)
    if core_ids is None:
        core_ids = list(range(len(in_maps)))
    return run_bass_kernel_spmd(b.nc, in_maps, core_ids=core_ids, **kw)


def emit_weight_precast(b):
    ctx = b.ctx
    slots = [b.big[:, 4096 * s:4096 * (s + 1)] for s in range(8)]
    slot_free = [None] * 8
    si = 0
    for name, shape in b.wlist:
        fp = b.dram[name]
        scr = b.dram["ws_" + name]
        rows = 1
        for d in shape[:-1]:
            rows *= d
        cols = shape[-1]
        fp2 = fp.rearrange(
            " ".join(f"a{i}" for i in range(len(shape))) + " -> (" +
            " ".join(f"a{i}" for i in range(len(shape) - 1)) + f") a{len(shape)-1}"
        ) if len(shape) > 1 else fp.rearrange("(o f) -> o f", o=1)
        sc2 = scr.rearrange(
            " ".join(f"a{i}" for i in range(len(shape))) + " -> (" +
            " ".join(f"a{i}" for i in range(len(shape) - 1)) + f") a{len(shape)-1}"
        ) if len(shape) > 1 else scr.rearrange("(o f) -> o f", o=1)
        r = 0
        while r < rows:
            nr = min(128, rows - r)
            ncol = cols
            assert ncol <= 4096
            s = si % 8
            si += 1
            w = [slot_free[s]] if slot_free[s] else []
            ld = ctx.dma("g", lambda e, s=s, r=r, nr=nr, ncol=ncol, fp2=fp2:
                         e.dma_start(out=slots[s][0:nr, 0:ncol],
                                     in_=fp2[r:r + nr, :]), waits=w)
            stp = ctx.dma("y", lambda e, s=s, r=r, nr=nr, ncol=ncol, sc2=sc2:
                          e.dma_start(out=sc2[r:r + nr, :],
                                      in_=slots[s][0:nr, 0:ncol]), waits=[ld])
            slot_free[s] = stp
            r += nr


def emit_all_preloads(b):
    ctx = b.ctx
    emit_preload_consts(b)
    b.din("l0_wqv", [D, 2 * D])
    bqv = b.din("l0_bqv", [2 * D])
    emit_load_bias_row(b, "l0_bqv", bqv, 2048)
    emit_load_bias_col(b, "l0_bo", b.din("L0_attn_bo", [D]), D)
    emit_load_bias_col(b, "l0_b1", b.din("L0_ffn_b1", [E0, F0]).rearrange(
        "a b -> (a b)"), E0 * F0)
    emit_load_bias_col(b, "l0_b2", b.din("L0_ffn_b2", [E0, D]).rearrange(
        "a b -> (a b)"), E0 * D)
    emit_l0_moe_weights_pre(b)
    emit_load_bias_col(b, "ln1_g", b.din("L1_attn_ln1_g", [D]), D)
    emit_load_bias_col(b, "ln1_b", b.din("L1_attn_ln1_b", [D]), D)
    emit_load_bias_col(b, "ln2_g", b.din("L1_attn_ln2_g", [D]), D)
    emit_load_bias_col(b, "ln2_b", b.din("L1_attn_ln2_b", [D]), D)
    for cname, dname, nf in [
            ("me_eq_b", "L1_ffn_me_eq_b", D), ("me_bv", "L1_ffn_me_bv", D),
            ("me_bo", "L1_ffn_me_bo", D), ("me_c1b", "L1_ffn_me_c1b", 2 * D),
            ("me_c2b", "L1_ffn_me_c2b", D),
            ("ce_syn_b", "L1_ffn_ce_syn_b", D), ("ce_bv", "L1_ffn_ce_bv", D),
            ("ce_bo", "L1_ffn_ce_bo", D), ("ce_n1g", "L1_ffn_ce_n1g", D),
            ("ce_n1b", "L1_ffn_ce_n1b", D), ("ce_f1b", "L1_ffn_ce_f1b", 2 * D),
            ("ce_f2b", "L1_ffn_ce_f2b", D), ("ce_n2g", "L1_ffn_ce_n2g", D),
            ("ce_n2b", "L1_ffn_ce_n2b", D), ("ce_gen_b", "L1_ffn_ce_gen_b", D),
            ("ml_b1", "L1_ffn_ml_b1", I1), ("ml_b2", "L1_ffn_ml_b2", D)]:
        emit_load_bias_col(b, cname, b.din(dname, [nf]), nf)
    rw1 = b.din("L1_ffn_router_w", [D, 4])
    ctx.dma("g", lambda e: e.dma_start(
        out=b.rwt1.rearrange("p (c e) -> p c e", e=4),
        in_=rw1.rearrange("(c p) e -> p c e", p=128)))


def build_full(nseq=SEQS):
    b = B()
    ctx = b.ctx
    x0 = b.din("x0", [SEQS, CH, 128, N])
    y = b.dout("y", [SEQS, CH, 128, N])
    x1h = b.dint("x1h", [SEQS, CH, 128, N])
    b.wsrc("l0_wqv", [D, 2 * D])
    b.wsrc("L0_attn_Wo", [D, D])
    b.wsrc("L0_ffn_w1", [E0, D, F0])
    b.wsrc("L0_ffn_w2", [E0, F0, D])
    b.wsrc("L1_attn_Wq", [D, D])
    b.wsrc("L1_attn_Wkv", [D, 128])
    b.wsrc("L1_attn_Wo", [D, D])
    for nm, sh in [("L1_ffn_sw_w1", [D, I1]), ("L1_ffn_sw_w3", [D, I1]),
                   ("L1_ffn_sw_w2", [I1, D]), ("L1_ffn_me_eq_w", [D, D]),
                   ("L1_ffn_me_wv", [D, D]), ("L1_ffn_me_wo", [D, D]),
                   ("L1_ffn_me_c1w", [D, 2 * D]), ("L1_ffn_me_c2w", [2 * D, D]),
                   ("L1_ffn_ce_syn_w", [D, D]), ("L1_ffn_ce_wv", [D, D]),
                   ("L1_ffn_ce_wo", [D, D]), ("L1_ffn_ce_f1w", [D, 2 * D]),
                   ("L1_ffn_ce_f2w", [2 * D, D]), ("L1_ffn_ce_gen_w", [D, D]),
                   ("L1_ffn_ml_w1", [D, I1]), ("L1_ffn_ml_w2", [I1, D])]:
        b.wsrc(nm, sh)
    emit_all_preloads(b)
    emit_weight_precast(b)
    ctx.barrier()
    for sq in range(nseq):
        wqv_tgts = emit_l0_attn_weights(b)
        xn_done = emit_outer_ln(b, x0[sq])
        s0_done = emit_l0_attn(b, xn_done, wqv_tgts, None)
        ctx.barrier()
        g0_done = emit_l0_moe(b, ctx.sv.n)
        ctx.barrier()
        emit_combine(b, x0[sq], x1h[sq], ctx.sv.n, ctx.sv.n)
        ctx.barrier()
        xn1_done = emit_outer_ln(b, x1h[sq])
        s1_done = emit_l1_attn(b, xn1_done)
        ctx.barrier()
        g1_done = emit_l1_ffn(b, ctx.sv.n)
        ctx.barrier()
        emit_combine(b, x1h[sq], y[sq], ctx.sv.n, ctx.sv.n)
        ctx.barrier()
    emit_final_drain(b)
    return b


def kernel(**inputs):
    x = np.asarray(inputs["L0_attn_x"], dtype=np.float32)
    wqv = np.ascontiguousarray(np.concatenate(
        [inputs["L0_attn_Wq"], inputs["L0_attn_Wv"]], axis=1).astype(np.float32))
    bqv = np.ascontiguousarray(np.concatenate(
        [inputs["L0_attn_bq"], inputs["L0_attn_bv"]]).astype(np.float32))
    shared = {"l0_wqv": wqv, "l0_bqv": bqv}
    for nm in ["L0_attn_Wo", "L0_attn_bo", "L0_ffn_router_w", "L0_ffn_w1",
               "L0_ffn_b1", "L0_ffn_w2", "L0_ffn_b2",
               "L1_attn_ln1_g", "L1_attn_ln1_b", "L1_attn_Wq", "L1_attn_Wkv",
               "L1_attn_Wo", "L1_attn_ln2_g", "L1_attn_ln2_b",
               "L1_ffn_router_w", "L1_ffn_router_b", "L1_ffn_load_balancer",
               "L1_ffn_sw_w1", "L1_ffn_sw_w3", "L1_ffn_sw_w2",
               "L1_ffn_me_eq_w", "L1_ffn_me_eq_b", "L1_ffn_me_wv",
               "L1_ffn_me_bv", "L1_ffn_me_wo", "L1_ffn_me_bo",
               "L1_ffn_me_c1w", "L1_ffn_me_c1b", "L1_ffn_me_c2w",
               "L1_ffn_me_c2b", "L1_ffn_ce_syn_w", "L1_ffn_ce_syn_b",
               "L1_ffn_ce_wv", "L1_ffn_ce_bv", "L1_ffn_ce_wo",
               "L1_ffn_ce_bo", "L1_ffn_ce_n1g", "L1_ffn_ce_n1b",
               "L1_ffn_ce_f1w", "L1_ffn_ce_f1b", "L1_ffn_ce_f2w",
               "L1_ffn_ce_f2b", "L1_ffn_ce_n2g", "L1_ffn_ce_n2b",
               "L1_ffn_ce_gen_w", "L1_ffn_ce_gen_b", "L1_ffn_ml_w1",
               "L1_ffn_ml_b1", "L1_ffn_ml_w2", "L1_ffn_ml_b2"]:
        shared[nm] = np.ascontiguousarray(np.asarray(inputs[nm], np.float32))
    in_maps = []
    for c in range(NCORES):
        xc = x[SEQS * c:SEQS * (c + 1)]
        xf = np.ascontiguousarray(xc.transpose(0, 2, 1)).reshape(
            SEQS, CH, 128, N)
        m = dict(shared)
        m["x0"] = xf
        in_maps.append(m)
    b = build_full()
    with b.nc.Block() as block:
        b.ctx.finalize(block)
    res = run_bass_kernel_spmd(b.nc, in_maps, core_ids=list(range(NCORES)))
    outs = []
    for c in range(NCORES):
        yc = np.asarray(res.results[c]["y"]).reshape(SEQS, D, N)
        outs.append(yc.transpose(0, 2, 1))
    return np.ascontiguousarray(np.concatenate(outs, axis=0))
